# version 05
# baseline (speedup 1.0000x reference)
import contextlib
import numpy as np
import ml_dtypes
import concourse.bass as bass
import concourse.mybir as mybir
from concourse.bass_utils import run_bass_kernel_spmd

F32 = mybir.dt.float32
BF16 = mybir.dt.bfloat16
AF = mybir.ActivationFunctionType
ALU = mybir.AluOpType

T = 2048
D = 1024
NT = 16
NCH = 8
EXPC = 8.0

SEMS = ["mod", "slc", "lns", "ln", "tr", "trev", "qmm", "qev", "s2m", "s2r", "s2f", "mtr", "mng", "pbs0", "pbs1",
        "vmm", "vev", "smm", "sexp", "spv", "srs", "sbc", "snrm", "womm",
        "oev", "etr", "eadd", "ln2s", "ln2", "tr2", "trev2", "gmm", "gsig",
        "gw", "fA", "fSl", "fG", "fB", "fEv", "wsc", "bar", "sto", "dbgs"]


class DPool:

    def __init__(self, nc, ctx, prefix, n):
        self.sems = [ctx.enter_context(nc.semaphore(prefix + str(i))) for i in range(n)]
        self.vals = [0] * n
        self.rr = 0

    def issue(self, eng, out, in_):
        k = self.rr
        self.rr = (self.rr + 1) % len(self.sems)
        if self.vals[k] > 0:
            eng.wait_ge(self.sems[k], self.vals[k])
        eng.dma_start(out=out, in_=in_).then_inc(self.sems[k], 16)
        self.vals[k] += 16
        return (self.sems[k], self.vals[k])


def qev_pref(ft):
    pat = [2, 1, 1]
    return sum(pat[f % 3] for f in range(ft))


QEV_CH = qev_pref(24)


def build(debug=()):
    nc = bass.Bass("TRN2", target_bir_lowering=False)

    x_ = nc.dram_tensor("x", [T, D], F32, kind="ExternalInput").ap()
    wqkv_ = nc.dram_tensor("wqkv", [3 * D, D], F32, kind="ExternalInput").ap()
    bqkv_ = nc.dram_tensor("bqkv", [128, 24], F32, kind="ExternalInput").ap()
    bqkvv_ = nc.dram_tensor("bqkvv", [1, D], F32, kind="ExternalInput").ap()
    wo_ = nc.dram_tensor("wo", [D, D], F32, kind="ExternalInput").ap()
    bo_ = nc.dram_tensor("bo", [128, 8], F32, kind="ExternalInput").ap()
    wg_ = nc.dram_tensor("wg", [128, 64], F32, kind="ExternalInput").ap()
    we1_ = nc.dram_tensor("we1", [9, D, 2048], F32, kind="ExternalInput").ap()
    we3_ = nc.dram_tensor("we3", [9, D, 2048], F32, kind="ExternalInput").ap()
    we2_ = nc.dram_tensor("we2", [9, 2048, D], F32, kind="ExternalInput").ap()
    cvec_ = nc.dram_tensor("cvec", [128, 8], F32, kind="ExternalInput").ap()
    wmod_ = nc.dram_tensor("wmod", [D, 6 * D], F32, kind="ExternalInput").ap()
    ident_ = nc.dram_tensor("ident", [128, 128], BF16, kind="ExternalInput").ap()
    identf_ = nc.dram_tensor("identf", [128, 128], F32, kind="ExternalInput").ap()
    out_ = nc.dram_tensor("out", [T, D], F32, kind="ExternalOutput").ap()
    dbg = {}
    for name, shape in debug:
        dbg[name] = nc.dram_tensor("dbg_" + name, list(shape), F32, kind="ExternalOutput").ap()

    ctx = contextlib.ExitStack()
    with ctx:
        E = ctx.enter_context

        x_sb = E(nc.sbuf_tensor("x_sb", [128, NT * D], F32))
        ident = E(nc.sbuf_tensor("identb_sb", [128, 128], BF16))
        identf = E(nc.sbuf_tensor("identf_sb", [128, 128], F32))
        cst = E(nc.sbuf_tensor("cst", [128, 8], F32))
        ones_bf = E(nc.sbuf_tensor("ones_bf", [1, 128], BF16))
        ones_f32 = E(nc.sbuf_tensor("ones_f32", [128, 1], F32))
        mod_sb = E(nc.sbuf_tensor("mod_sb", [128, 48], F32))
        ca1 = E(nc.sbuf_tensor("ca1", [128, 8], F32))
        cb1 = E(nc.sbuf_tensor("cb1", [128, 8], F32))
        ga_v = E(nc.sbuf_tensor("ga_v", [128, 8], F32))
        gb_v = E(nc.sbuf_tensor("gb_v", [128, 8], F32))
        gabo_v = E(nc.sbuf_tensor("gabo_v", [128, 8], F32))
        bo_fm = E(nc.sbuf_tensor("bo_fm", [128, 8], F32))
        t8a = E(nc.sbuf_tensor("t8a", [128, 8], F32))
        t8b = E(nc.sbuf_tensor("t8b", [128, 8], F32))
        c_fm = E(nc.sbuf_tensor("c_fm", [128, 8], F32))
        silc = E(nc.sbuf_tensor("silc", [128, 8], F32))
        bqkv_sb = E(nc.sbuf_tensor("bqkv_sb", [128, 24], F32))
        qmask = E(nc.sbuf_tensor("qmask", [128, 24], F32))
        bq_eff = E(nc.sbuf_tensor("bq_eff", [128, 24], F32))
        bqv_row = E(nc.sbuf_tensor("bqv_row", [1, D], F32))
        bqv_bf = E(nc.sbuf_tensor("bqv_bf", [1, D], BF16))
        wg_sb = E(nc.sbuf_tensor("wg_sb", [128, 64], BF16))
        st6 = E(nc.sbuf_tensor("st6", [128, NT * 12], F32))
        mv = E(nc.sbuf_tensor("mv", [128, NT * 2], F32))
        lnv = E(nc.sbuf_tensor("lnv", [128, NT], F32))
        r_all = E(nc.sbuf_tensor("r_all", [128, NT], F32))

        sems = {name: E(nc.semaphore("s_" + name)) for name in SEMS}
        V = {name: 0 for name in SEMS}
        pool_s = DPool(nc, ctx, "ps_", 6)
        pool_g = DPool(nc, ctx, "pg_", 6)
        tok = {}

        def twait(eng, key):
            sem, val = tok[key]
            eng.wait_ge(sem, val)

        block = E(nc.Block())
        nbar = [0]

        def bar_all():
            nbar[0] += 1
            k = nbar[0]
            for sec in (block.tensor, block.scalar, block.vector):
                def mk(eng, _k=k):
                    eng.drain().then_inc(sems["bar"], 1)
                    eng.wait_ge(sems["bar"], 5 * _k)
                sec(mk)
            for sec in (block.gpsimd, block.sync):
                def mk2(eng, _k=k):
                    eng.nop().then_inc(sems["bar"], 1)
                    eng.wait_ge(sems["bar"], 5 * _k)
                sec(mk2)
            V["bar"] = 5 * k

        @block.sync
        def _(sync):
            tok["x"] = pool_s.issue(sync, x_sb[:].rearrange("p (n f) -> p n f", f=D),
                                    x_.rearrange("(n p) f -> p n f", p=128))
            tok["ident"] = pool_s.issue(sync, ident[:], ident_)
            tok["identf"] = pool_s.issue(sync, identf[:], identf_)
            tok["cfm"] = pool_s.issue(sync, c_fm[:], cvec_)
            tok["bqkv"] = pool_s.issue(sync, bqkv_sb[:], bqkv_)
            tok["bqvrow"] = pool_s.issue(sync, bqv_row[:], bqkvv_)
            tok["bofm"] = pool_s.issue(sync, bo_fm[:], bo_)

        @block.gpsimd
        def _(g):
            tok["wg"] = pool_g.issue(g, wg_sb[:], wg_)

        @block.vector
        def _(v):
            v.memset(cst[:, 0:1], -1.0)
            v.memset(cst[:, 1:2], 0.125)
            v.memset(cst[:, 2:3], 1e-5)
            v.memset(cst[:, 3:4], -0.5)
            v.memset(cst[:, 4:5], 1.0 / 3.0)
            v.memset(cst[:, 5:6], 1.0)
            v.memset(cst[:, 6:7], 0.25)
            v.memset(ones_bf[:], 1.0)
            v.memset(ones_f32[:], 1.0)
            v.memset(qmask[0:64, 0:24:3], 0.125)
            v.memset(qmask[64:128, 0:24:3], 1.0)
            v.memset(qmask[0:64, 1:24:3], 1.0)
            v.memset(qmask[64:128, 1:24:3], 0.125)
            v.memset(qmask[:, 2:24:3], 1.0)
            twait(v, "bqkv")
            twait(v, "bqvrow")
            v.drain()
            v.tensor_mul(bq_eff[:], bqkv_sb[:], qmask[:])
            v.tensor_copy(bqv_bf[:], bqv_row[:])
            v.drain().then_inc(sems["mod"], 1)
        V["mod"] += 1

        @block.vector
        def _(v):
            twait(v, "x")
            for i in range(NT):
                v.bn_stats(st6[:, i * 12:i * 12 + 6], x_sb[:, i * D:i * D + 512])
                v.bn_stats(st6[:, i * 12 + 6:i * 12 + 12], x_sb[:, i * D + 512:(i + 1) * D])
            v.drain()
            for i in range(NT):
                v.bn_aggr(mv[:, i * 2:(i + 1) * 2], st6[:, i * 12:(i + 1) * 12])
            v.drain().then_inc(sems["lns"], 1)
        V["lns"] += 1

        @block.scalar
        def _(s):
            s.wait_ge(sems["lns"], 1)
            s.wait_ge(sems["mod"], 1)
            var_view = mv[:].rearrange("p (n two) -> p n two", two=2)[:, :, 1:2]
            s.activation(lnv[:], var_view, AF.Ln, bias=cst[:, 2:3])
            s.drain()
            s.activation(r_all[:], lnv[:], AF.Exp, scale=cst[:, 3:4]).then_inc(sems["lns"], 1)
        V["lns"] += 1

        wscope = contextlib.ExitStack()
        wqkvT = wscope.enter_context(nc.sbuf_tensor("wqkvT", [128, 8 * 3072], BF16))
        woT = wscope.enter_context(nc.sbuf_tensor("woT", [128, 8 * 1024], BF16))

        with nc.sbuf_tensor("wnat", [128, 8 * 1024], BF16) as wnat, \
             nc.psum_tensor("trp0", [128, 1024], BF16) as trp0, \
             nc.psum_tensor("trp1", [128, 1024], BF16) as trp1:

            @block.gpsimd
            def _(g):
                for grp in range(4):
                    if grp >= 1:
                        g.wait_ge(sems["trev"], (grp - 1) * 8 + 8)
                    if grp < 3:
                        src = wqkv_.rearrange("(g j p) d -> g p j d", p=128, j=8)[grp]
                    else:
                        src = wo_.rearrange("(j p) d -> p j d", p=128)
                    tok["wnat", grp] = pool_g.issue(g, wnat[:].rearrange("p (j d) -> p j d", d=D), src)

            @block.tensor
            def _(t):
                twait(t, "ident")
                for grp in range(4):
                    twait(t, ("wnat", grp))
                    for j in range(8):
                        bank = trp0 if j % 2 == 0 else trp1
                        idx = grp * 8 + j
                        if idx >= 2:
                            t.wait_ge(sems["trev"], idx - 1)
                        for k in range(8):
                            t.transpose(bank[:, k * 128:(k + 1) * 128],
                                        wnat[:, j * 1024 + k * 128: j * 1024 + (k + 1) * 128],
                                        ident[:]).then_inc(sems["tr"], 1)
            V["tr"] += 256
            TRW = V["tr"]

            @block.scalar
            def _(s):
                for grp in range(4):
                    for j in range(8):
                        bank = trp0 if j % 2 == 0 else trp1
                        idx = grp * 8 + j
                        s.wait_ge(sems["tr"], 8 * idx + 8)
                        if grp < 3:
                            ft = grp * 8 + j
                            dst = wqkvT[:].rearrange("p (k f) -> p k f", f=3072)[:, :, ft * 128:(ft + 1) * 128]
                        else:
                            dst = woT[:].rearrange("p (k f) -> p k f", f=1024)[:, :, j * 128:(j + 1) * 128]
                        s.activation(dst, bank[:].rearrange("p (k f) -> p k f", f=128),
                                     AF.Copy).then_inc(sems["trev"], 1)
            V["trev"] += 32
            TRV0 = V["trev"]

        with nc.sbuf_tensor("wmbuf", [128, 2 * 3072], F32) as wmbuf, \
             nc.sbuf_tensor("macc", [128, 6144], F32) as macc, \
             nc.psum_tensor("mred", [128, 64], F32) as mred:

            @block.scalar
            def _(s):
                twait(s, "cfm")
                s.activation(silc[:], c_fm[:], AF.Silu).then_inc(sems["slc"], 1)
            V["slc"] += 1

            MODB = V["mod"] + 1

            @block.sync
            def _(sync):
                sync.wait_ge(sems["tr"], TRW)
                for h in range(16):
                    if h >= 2:
                        sync.wait_ge(sems["mod"], MODB + 4 * (h - 1))
                    k, hh = divmod(h, 2)
                    tok["wm", h] = pool_s.issue(sync, wmbuf[:, (h % 2) * 3072:(h % 2 + 1) * 3072],
                                                wmod_[k * 128:(k + 1) * 128, hh * 3072:(hh + 1) * 3072])

            @block.vector
            def _(v):
                v.memset(macc[:], 0.0).then_inc(sems["mod"], 1)
                v.wait_ge(sems["slc"], 1)
                v.wait_ge(sems["mod"], MODB)
                for h in range(16):
                    k, hh = divmod(h, 2)
                    twait(v, ("wm", h))
                    for q in range(4):
                        dsl = slice(hh * 3072 + q * 768, hh * 3072 + (q + 1) * 768)
                        ssl = slice((h % 2) * 3072 + q * 768, (h % 2) * 3072 + (q + 1) * 768)
                        v.scalar_tensor_tensor(macc[:, dsl], wmbuf[:, ssl], silc[:, k:k + 1],
                                               macc[:, dsl], ALU.mult, ALU.add).then_inc(sems["mod"], 1)
            V["mod"] += 1 + 64

            @block.tensor
            def _(t):
                t.wait_ge(sems["mod"], V["mod"])
                t.wait_ge(sems["trev"], TRV0)
                for col in range(48):
                    t.matmul(mred[:, col:col + 1], macc[:, col * 128:(col + 1) * 128],
                             ones_f32[:], start=True, stop=True)
                t.drain().then_inc(sems["mod"], 1)
            V["mod"] += 1

            @block.vector
            def _(v):
                v.wait_ge(sems["mod"], V["mod"])
                v.tensor_copy(mod_sb[:], mred[:, 0:48])
                v.drain()
                ca = mod_sb[:, 8:16]
                ga = mod_sb[:, 16:24]
                cb = mod_sb[:, 32:40]
                gb = mod_sb[:, 40:48]
                v.tensor_scalar(t8a[:], ga, -1.0, None, ALU.mult)
                v.tensor_scalar(t8b[:], gb, -1.0, None, ALU.mult)
                v.tensor_scalar(ca1[:], ca, 1.0, None, ALU.add)
                v.tensor_scalar(cb1[:], cb, 1.0, None, ALU.add)
                v.drain()
                v.tensor_max(t8a[:], ga, t8a[:])
                v.tensor_max(t8b[:], gb, t8b[:])
                v.tensor_scalar(ga_v[:], ga, 0.0, None, ALU.is_ge)
                v.tensor_scalar(gb_v[:], gb, 0.0, None, ALU.is_ge)
                v.drain()
                v.tensor_scalar(t8a[:], t8a[:], 0.25, None, ALU.max)
                v.tensor_scalar(t8b[:], t8b[:], 0.25, None, ALU.max)
                v.tensor_scalar(ga_v[:], ga_v[:], 2.0, -1.0, ALU.mult, ALU.add)
                v.tensor_scalar(gb_v[:], gb_v[:], 2.0, -1.0, ALU.mult, ALU.add)
                v.drain()
                v.tensor_mul(ga_v[:], ga_v[:], t8a[:])
                v.tensor_mul(gb_v[:], gb_v[:], t8b[:])
                v.drain()
                twait(v, "bofm")
                v.tensor_mul(gabo_v[:], ga_v[:], bo_fm[:])
                v.drain().then_inc(sems["mod"], 1)
            V["mod"] += 1
        MOD_DONE = V["mod"]

        bar_all()

        if "mod48" in dbg:
            @block.sync
            def _(sync):
                tok["d_mod48"] = pool_s.issue(sync, dbg["mod48"], mod_sb[:])

        ascope = contextlib.ExitStack()
        hn1 = ascope.enter_context(nc.sbuf_tensor("hn1", [128, 4 * D], F32))
        h1c = ascope.enter_context(nc.sbuf_tensor("h1c", [128, 8 * 256], BF16))
        Qb = ascope.enter_context(nc.sbuf_tensor("Qb", [65, 4096], BF16))
        Kb = ascope.enter_context(nc.sbuf_tensor("Kb", [65, 4096], BF16))
        MP4 = ascope.enter_context(nc.sbuf_tensor("MP4", [128, 128], F32))
        M_part = ascope.enter_context(nc.sbuf_tensor("M_part", [128, 32], F32))
        Vtm = ascope.enter_context(nc.sbuf_tensor("Vtm", [128, 32 * 65], BF16))
        Eb = ascope.enter_context(nc.sbuf_tensor("Eb", [128, 3 * 512], BF16))
        att = ascope.enter_context(nc.sbuf_tensor("att", [128, 8 * 256], BF16))
        t1c = ascope.enter_context(nc.sbuf_tensor("t1c", [128, 8 * 256], F32))
        rsb = ascope.enter_context(nc.sbuf_tensor("rsb", [1, 2 * 512], F32))
        Pbs = ascope.enter_context(nc.sbuf_tensor("Pbs", [64, 2 * 512], F32))
        PS = ascope.enter_context(nc.psum_tensor("PS", [128, 4096], F32))
        R = [PS[:, i * 512:(i + 1) * 512] for i in range(8)]

        @block.vector
        def _(v):
            v.memset(Vtm[:], 1.0)
            v.memset(Kb[64:65, :], 1.0)
            v.drain().then_inc(sems["vev"], 1)
        V["vev"] += 1
        VEV0 = V["vev"]


        @block.tensor
        def _(t):
            t.wait_ge(sems["bar"], V["bar"])
            t.wait_ge(sems["mod"], MOD_DONE)
            twait(t, "identf")
            for c in range(NCH):
                t.wait_ge(sems["ln"], c // 2 + 1)
                if c >= 1:
                    t.wait_ge(sems["s2r"], 128 * c)
                for u in range(2):
                    for k in range(8):
                        idx = u * 8 + k
                        if idx >= 4:
                            t.wait_ge(sems["trev"], TRV0 + 16 * c + idx - 3)
                        ti = (2 * c + u) % 4
                        t.transpose(R[idx % 4][:, 0:128],
                                    hn1[:, ti * D + k * 128: ti * D + (k + 1) * 128],
                                    identf[:]).then_inc(sems["tr"], 1)
                V["tr"] += 16
                t.wait_ge(sems["trev"], TRV0 + 16 * (c + 1))
                if c >= 1:
                    t.wait_ge(sems["eadd"], 4 * c)
                for ft in range(24):
                    bank = R[4 + ft % 2]
                    if ft >= 2:
                        t.wait_ge(sems["qev"], QEV_CH * c + qev_pref(ft - 1))
                    mm = None
                    for k in range(8):
                        mm = t.matmul(bank[:, 0:256],
                                      wqkvT[:, k * 3072 + ft * 128: k * 3072 + (ft + 1) * 128],
                                      h1c[:, k * 256:(k + 1) * 256],
                                      start=(k == 0), stop=(k == 7))
                    mm.then_inc(sems["qmm"], 1)
                V["qmm"] += 24
                for vb in range(4):
                    u, half = vb // 2, vb % 2
                    bank = R[6 + vb % 2]
                    if vb >= 2:
                        t.wait_ge(sems["vev"], VEV0 + 4 * c + vb - 1)
                    elif c > 0:
                        t.wait_ge(sems["snrm"], 16 * c)
                    wq4 = wqkvT[:].rearrange("p (k G seg) -> p k G seg", G=16, seg=192)
                    for k in range(8):
                        t.matmul(bank[:, 0:512],
                                 h1c[:, k * 256 + u * 128: k * 256 + u * 128 + 128],
                                 wq4[:, k, half * 8:(half + 1) * 8, 128:192],
                                 start=(k == 0), stop=False)
                    t.matmul(bank[:, 0:512], ones_bf[:],
                             bqv_bf[0:1, half * 512:(half + 1) * 512],
                             start=False, stop=True).then_inc(sems["vmm"], 1)
                V["vmm"] += 4
                t.wait_ge(sems["qev"], QEV_CH * (c + 1))
                t.wait_ge(sems["vev"], VEV0 + 4 * (c + 1))
                for qt in range(32):
                    for pc in range(4):
                        b = 4 * qt + pc
                        if b >= 2:
                            t.wait_ge(sems["s2r"], 128 * c + b - 1)
                        for nb in range(2):
                            t.matmul(PS[:, (b % 2) * 1024 + nb * 512:(b % 2) * 1024 + (nb + 1) * 512],
                                     Qb[0:64, qt * 128:(qt + 1) * 128],
                                     Kb[0:64, pc * 1024 + nb * 512: pc * 1024 + (nb + 1) * 512],
                                     start=True, stop=True).then_inc(sems["s2m"], 1)
                V["s2m"] += 256
                for sl in range(8):
                    g = 8 * c + sl
                    Ob = R[6 + sl % 2]
                    TK0 = 32 * sl
                    for i in range(4):
                        qt = 4 * sl + i
                        sp = R[4 + (TK0 + i) % 2]
                        t.wait_ge(sems["s2f"], 32 * c + qt + 1)
                        if sl >= 1 and i < 2:
                            t.wait_ge(sems["sexp"], 256 * c + TK0 + i - 1)
                        if i >= 2:
                            t.wait_ge(sems["mng"], 32 * c + qt - 1)
                        t.transpose(sp[0:1, 0:128], M_part[:, qt:qt + 1],
                                    identf[:]).then_inc(sems["mtr"], 1)
                    for jt in range(32):
                        TK = 32 * sl + jt
                        Sp = R[4 + TK % 2]
                        if TK >= 2:
                            t.wait_ge(sems["sexp"], 256 * c + TK - 1)
                        if jt == 0:
                            t.wait_ge(sems["mng"], 32 * c + 4 * (sl + 1))
                            if g >= 2:
                                t.wait_ge(sems["snrm"], 2 * (g - 1))
                        t.matmul(Sp[:, 0:512], Kb[:, jt * 128:(jt + 1) * 128],
                                 Qb[:, sl * 512:(sl + 1) * 512],
                                 start=True, stop=True).then_inc(sems["smm"], 1)
                        t.wait_ge(sems["sexp"], 256 * c + TK + 1)
                        t.matmul(Ob[0:65, 0:512], Vtm[:, jt * 65: jt * 65 + 65],
                                 Eb[:, (TK % 3) * 512:(TK % 3) * 512 + 512],
                                 start=(jt == 0), stop=(jt == 31)).then_inc(sems["spv"], 1)
                    V["smm"] += 32
                    V["spv"] += 32
                V["mtr"] += 32
                t.wait_ge(sems["snrm"], 16 * (c + 1))
                t.wait_ge(sems["sexp"], 256 * (c + 1))
                for j in range(8):
                    bank = R[4 + j % 4]
                    if j >= 4:
                        t.wait_ge(sems["oev"], 8 * c + j - 3)
                    mm = None
                    for ki in range(8):
                        mm = t.matmul(bank[:, 0:256],
                                      woT[:, ki * 1024 + j * 128: ki * 1024 + (j + 1) * 128],
                                      att[:, ki * 256:(ki + 1) * 256],
                                      start=(ki == 0), stop=(ki == 7))
                    mm.then_inc(sems["womm"], 1)
                V["womm"] += 8
                t.wait_ge(sems["oev"], 8 * (c + 1))
                for grp in range(4):
                    uh, jh = grp // 2, grp % 2
                    bank = R[4 + grp % 2]
                    if grp >= 2:
                        t.wait_ge(sems["eadd"], 4 * c + grp - 1)
                    for l in range(4):
                        j = jh * 4 + l
                        t.transpose(bank[:, l * 128:(l + 1) * 128],
                                    t1c[:, j * 256 + uh * 128: j * 256 + uh * 128 + 128],
                                    identf[:]).then_inc(sems["etr"], 1)
                V["etr"] += 16

        @block.scalar
        def _(s):
            s.wait_ge(sems["bar"], V["bar"])
            s.wait_ge(sems["mod"], MOD_DONE)
            for c in range(NCH):
                for u in range(2):
                    for k in range(8):
                        idx = u * 8 + k
                        s.wait_ge(sems["tr"], TRW + 16 * c + idx + 1)
                        s.activation(h1c[:, k * 256 + u * 128: k * 256 + u * 128 + 128],
                                     R[idx % 4][:, 0:128], AF.Identity,
                                     scale=ca1[:, k:k + 1],
                                     bias=mod_sb[:, k:k + 1]).then_inc(sems["trev"], 1)
                for sl in range(8):
                    g = 8 * c + sl
                    TK0 = 32 * sl
                    for i in range(4):
                        qt = 4 * sl + i
                        sp = R[4 + (TK0 + i) % 2]
                        s.wait_ge(sems["mtr"], 32 * c + qt + 1)
                        s.activation(Qb[64:65, qt * 128:(qt + 1) * 128],
                                     sp[0:1, 0:128], AF.Identity,
                                     scale=cst[0:1, 0:1]).then_inc(sems["mng"], 1)
                    for jt in range(32):
                        TK = 32 * sl + jt
                        Sp = R[4 + TK % 2]
                        s.wait_ge(sems["smm"], 256 * c + TK + 1)
                        if TK >= 3:
                            s.wait_ge(sems["spv"], 256 * c + TK - 2)
                        s.activation(Eb[:, (TK % 3) * 512:(TK % 3) * 512 + 512],
                                     Sp[:, 0:512], AF.Exp).then_inc(sems["sexp"], 1)
                for j in range(8):
                    bank = R[4 + j % 4]
                    s.wait_ge(sems["womm"], 8 * c + j + 1)
                    s.activation(t1c[:, j * 256:(j + 1) * 256], bank[:, 0:256],
                                 AF.Identity, scale=ga_v[:, j:j + 1],
                                 bias=gabo_v[:, j:j + 1]).then_inc(sems["oev"], 1)
        V["trev"] += 16 * NCH
        V["mng"] += 32 * NCH
        V["sexp"] += 256 * NCH
        V["oev"] += 8 * NCH

        @block.gpsimd
        def _(g_):
            g_.wait_ge(sems["bar"], V["bar"])
            g_.wait_ge(sems["mod"], MOD_DONE)
            for c in range(NCH):
                for sl in range(8):
                    g = 8 * c + sl
                    off = (g % 2) * 512
                    g_.wait_ge(sems["srs"], g + 1)
                    if g >= 2:
                        g_.wait_ge(sems["snrm"], 2 * (g - 1))
                    g_.dma_start(
                        out=Pbs[:, off:off + 512],
                        in_=rsb[0:1, off:off + 512].unsqueeze(1).to_broadcast((1, 64, 512)),
                    ).then_inc(sems["pbs" + str(g % 2)], 16)
        V["pbs0"] += 16 * 4 * NCH
        V["pbs1"] += 16 * 4 * NCH

        @block.vector
        def _(v):
            v.wait_ge(sems["bar"], V["bar"])
            v.wait_ge(sems["mod"], MOD_DONE)
            v.wait_ge(sems["vev"], VEV0)
            for c in range(NCH):
                if c % 2 == 0:
                    cp = c // 2
                    v.wait_ge(sems["lns"], 2)
                    if cp >= 1:
                        v.wait_ge(sems["tr"], TRW + 32 * cp)
                    for q in range(4):
                        i = 4 * cp + q
                        v.tensor_scalar(hn1[:, q * D:(q + 1) * D], x_sb[:, i * D:(i + 1) * D],
                                        mv[:, i * 2:i * 2 + 1], r_all[:, i:i + 1],
                                        ALU.subtract, ALU.mult)
                    v.drain().then_inc(sems["ln"], 1)
                for ft in range(24):
                    bank = R[4 + ft % 2]
                    G0 = (ft // 3) * 2
                    v.wait_ge(sems["qmm"], 24 * c + ft + 1)
                    if ft % 3 == 0:
                        v.tensor_scalar(Qb[0:64, G0 * 256:(G0 + 1) * 256], bank[0:64, 0:256],
                                        cst[0:64, 1:2], bq_eff[0:64, ft:ft + 1],
                                        ALU.mult, ALU.add).then_inc(sems["qev"], 1)
                        v.tensor_scalar(Kb[0:64, G0 * 256:(G0 + 1) * 256], bank[64:128, 0:256],
                                        bq_eff[64:128, ft:ft + 1], None,
                                        ALU.add).then_inc(sems["qev"], 1)
                    elif ft % 3 == 1:
                        v.tensor_scalar(Qb[0:64, (G0 + 1) * 256:(G0 + 2) * 256], bank[64:128, 0:256],
                                        cst[64:128, 1:2], bq_eff[64:128, ft:ft + 1],
                                        ALU.mult, ALU.add).then_inc(sems["qev"], 1)
                    else:
                        v.tensor_scalar(Kb[0:64, (G0 + 1) * 256:(G0 + 2) * 256], bank[0:64, 0:256],
                                        bq_eff[0:64, ft:ft + 1], None,
                                        ALU.add).then_inc(sems["qev"], 1)
                V["qev"] += QEV_CH
                for vb in range(4):
                    u, half = vb // 2, vb % 2
                    bank = R[6 + vb % 2]
                    v.wait_ge(sems["vmm"], 4 * c + vb + 1)
                    vt = Vtm[:].rearrange("p (t f) -> p t f", f=65)
                    dstv = vt[:, 16 * half + u: 16 * (half + 1): 2, 0:64]
                    v.tensor_copy(dstv, bank[:, 0:512].rearrange("p (g f) -> p g f", f=64)).then_inc(sems["vev"], 1)
                V["vev"] += 4
                for qt in range(32):
                    for pc in range(4):
                        b = 4 * qt + pc
                        v.wait_ge(sems["s2m"], 256 * c + 2 * (b + 1))
                        v.reduce_max(MP4[:, b:b + 1],
                                     PS[:, (b % 2) * 1024:(b % 2) * 1024 + 1024],
                                     axis=mybir.AxisListType.X).then_inc(sems["s2r"], 1)
                    v.wait_ge(sems["s2r"], 128 * c + 4 * (qt + 1))
                    v.reduce_max(M_part[:, qt:qt + 1], MP4[:, 4 * qt:4 * qt + 4],
                                 axis=mybir.AxisListType.X).then_inc(sems["s2f"], 1)
                V["s2r"] += 128
                V["s2f"] += 32
                for sl in range(8):
                    g = 8 * c + sl
                    Ob = R[6 + sl % 2]
                    v.wait_ge(sems["spv"], 256 * c + 32 * (sl + 1))
                    if g >= 2:
                        v.wait_ge(sems["pbs" + str(g % 2)], 16 * (g // 2))
                    v.reciprocal(rsb[0:1, (g % 2) * 512:(g % 2) * 512 + 512],
                                 Ob[64:65, 0:512]).then_inc(sems["srs"], 1)
                    if sl >= 1:
                        gq = g - 1
                        Obq = R[6 + (sl - 1) % 2]
                        v.wait_ge(sems["pbs" + str(gq % 2)], 16 * (gq // 2 + 1))
                        v.tensor_mul(att[0:64, (sl - 1) * 256:sl * 256],
                                     Obq[0:64, 0:256],
                                     Pbs[:, (gq % 2) * 512:(gq % 2) * 512 + 256]).then_inc(sems["snrm"], 1)
                        v.tensor_mul(att[64:128, (sl - 1) * 256:sl * 256],
                                     Obq[0:64, 256:512],
                                     Pbs[:, (gq % 2) * 512 + 256:(gq % 2) * 512 + 512]).then_inc(sems["snrm"], 1)
                        V["snrm"] += 2
                V["srs"] += 8
                g = 8 * c + 7
                v.wait_ge(sems["pbs" + str(g % 2)], 16 * (g // 2 + 1))
                v.tensor_mul(att[0:64, 7 * 256:8 * 256], R[7][0:64, 0:256],
                             Pbs[:, (g % 2) * 512:(g % 2) * 512 + 256]).then_inc(sems["snrm"], 1)
                v.tensor_mul(att[64:128, 7 * 256:8 * 256], R[7][0:64, 256:512],
                             Pbs[:, (g % 2) * 512 + 256:(g % 2) * 512 + 512]).then_inc(sems["snrm"], 1)
                V["snrm"] += 2
                for grp in range(4):
                    uh, jh = grp // 2, grp % 2
                    bank = R[4 + grp % 2]
                    v.wait_ge(sems["etr"], 16 * c + 4 * (grp + 1))
                    v.tensor_add(x_sb[:, (2 * c + uh) * D + jh * 512:(2 * c + uh) * D + jh * 512 + 512],
                                 x_sb[:, (2 * c + uh) * D + jh * 512:(2 * c + uh) * D + jh * 512 + 512],
                                 bank[:, 0:512]).then_inc(sems["eadd"], 1)
                V["eadd"] += 4
        V["ln"] += 4

        bar_all()
        ascope.close()
        wscope.close()

        if "xmid" in dbg:
            @block.sync
            def _(sync):
                tok["d_xmid"] = pool_s.issue(sync, dbg["xmid"].rearrange("(n p) f -> p n f", p=128),
                                             x_sb[:].rearrange("p (n f) -> p n f", f=D))
                tok["d_xmid_done"] = tok["d_xmid"]

        fscope = contextlib.ExitStack()
        h2fm = fscope.enter_context(nc.sbuf_tensor("h2fm", [128, 8 * 2048], BF16))
        we1h = fscope.enter_context(nc.sbuf_tensor("we1h", [128, 8 * 1024], BF16))
        we3h = fscope.enter_context(nc.sbuf_tensor("we3h", [128, 8 * 1024], BF16))
        we2h = fscope.enter_context(nc.sbuf_tensor("we2h", [128, 8 * 1024], BF16))
        gb_bc = fscope.enter_context(nc.sbuf_tensor("gb_bc", [128, D], BF16))
        g8 = fscope.enter_context(nc.sbuf_tensor("g8", [1, 1024], F32))
        g8b = fscope.enter_context(nc.sbuf_tensor("g8b", [1, 1024], BF16))
        H1 = fscope.enter_context(nc.psum_tensor("H1", [128, 2048], F32))
        H3 = fscope.enter_context(nc.psum_tensor("H3", [128, 2048], F32))

        H34 = [(H3, 0), (H3, 512), (H3, 1024), (H3, 1536)]

        @block.vector
        def _(v):
            for i in range(NT):
                v.bn_stats(st6[:, i * 12:i * 12 + 6], x_sb[:, i * D:i * D + 512])
                v.bn_stats(st6[:, i * 12 + 6:i * 12 + 12], x_sb[:, i * D + 512:(i + 1) * D])
            v.drain()
            for i in range(NT):
                v.bn_aggr(mv[:, i * 2:(i + 1) * 2], st6[:, i * 12:(i + 1) * 12])
            v.drain().then_inc(sems["ln2s"], 1)
        V["ln2s"] += 1

        @block.scalar
        def _(s):
            s.wait_ge(sems["ln2s"], 1)
            var_view = mv[:].rearrange("p (n two) -> p n two", two=2)[:, :, 1:2]
            s.activation(lnv[:], var_view, AF.Ln, bias=cst[:, 2:3])
            s.drain()
            s.activation(r_all[:], lnv[:], AF.Exp, scale=cst[:, 3:4]).then_inc(sems["ln2s"], 1)
        V["ln2s"] += 1

        @block.tensor
        def _(t):
            for k in range(8):
                t.transpose(H1[0:1, k * 128:(k + 1) * 128], gb_v[:, k:k + 1], identf[:])
            t.drain().then_inc(sems["gw"], 1)
        V["gw"] += 1

        @block.vector
        def _(v):
            v.wait_ge(sems["gw"], 1)
            v.tensor_copy(g8[:], H1[0:1, 0:1024])
            v.drain()
            v.tensor_copy(g8b[:], g8[:])
            v.drain().then_inc(sems["gw"], 1)
        V["gw"] += 1

        @block.tensor
        def _(t):
            t.wait_ge(sems["gw"], 2)
            for k in range(8):
                bank, off = (H3, (k % 4) * 512)
                t.matmul(bank[:, off + (k // 4) * 128: off + (k // 4) * 128 + 128],
                         ones_bf[:], g8b[0:1, k * 128:(k + 1) * 128], start=True, stop=True)
            t.drain().then_inc(sems["gw"], 1)
        V["gw"] += 1

        @block.vector
        def _(v):
            v.wait_ge(sems["gw"], 3)
            for k in range(8):
                bank, off = (H3, (k % 4) * 512)
                v.tensor_copy(gb_bc[:, k * 128:(k + 1) * 128],
                              bank[:, off + (k // 4) * 128: off + (k // 4) * 128 + 128])
            v.drain().then_inc(sems["gw"], 1)
        V["gw"] += 1
        GW0 = V["gw"]

        with nc.sbuf_tensor("hn2", [128, 4 * D], F32) as hn2:
            @block.vector
            def _(v):
                v.wait_ge(sems["ln2s"], 2)
                v.wait_ge(sems["gw"], GW0)
                for cp in range(4):
                    if cp >= 1:
                        v.wait_ge(sems["tr2"], 32 * cp)
                    for q in range(4):
                        i = 4 * cp + q
                        v.tensor_scalar(hn2[:, q * D:(q + 1) * D], x_sb[:, i * D:(i + 1) * D],
                                        mv[:, i * 2:i * 2 + 1], r_all[:, i:i + 1],
                                        ALU.subtract, ALU.mult)
                    v.drain().then_inc(sems["ln2"], 1)
            V["ln2"] += 4

            @block.tensor
            def _(t):
                t.wait_ge(sems["gw"], GW0)
                for cp in range(4):
                    t.wait_ge(sems["ln2"], cp + 1)
                    for k in range(8):
                        grp = cp * 8 + k
                        bank, off = H34[grp % 4]
                        if grp >= 4:
                            t.wait_ge(sems["trev2"], grp - 3)
                        for q in range(4):
                            t.transpose(bank[:, off + q * 128: off + (q + 1) * 128],
                                        hn2[:, q * D + k * 128: q * D + (k + 1) * 128],
                                        identf[:]).then_inc(sems["tr2"], 1)
            V["tr2"] += 128

            @block.scalar
            def _(s):
                for cp in range(4):
                    for k in range(8):
                        grp = cp * 8 + k
                        bank, off = H34[grp % 4]
                        s.wait_ge(sems["tr2"], 4 * (grp + 1))
                        s.activation(h2fm[:, k * 2048 + cp * 512: k * 2048 + (cp + 1) * 512],
                                     bank[:, off:off + 512], AF.Identity,
                                     scale=cb1[:, k:k + 1],
                                     bias=mod_sb[:, 24 + k:24 + k + 1]).then_inc(sems["trev2"], 1)
            V["trev2"] += 32

        with nc.sbuf_tensor("Gbuf", [128, 8 * 2048], BF16) as Gbuf, \
             nc.sbuf_tensor("gates", [128, NT * 8], F32) as gates, \
             nc.sbuf_tensor("top8", [128, NT * 8], F32) as top8, \
             nc.sbuf_tensor("gsum", [128, NT], F32) as gsum, \
             nc.sbuf_tensor("grcp", [128, NT], F32) as grcp, \
             nc.sbuf_tensor("gm1", [128, NT * 8], F32) as gm1, \
             nc.sbuf_tensor("gm2", [128, NT * 8], F32) as gm2, \
             nc.sbuf_tensor("w_sb", [128, NT * 8], F32) as w_sb:

            @block.tensor
            def _(t):
                t.wait_ge(sems["trev2"], 32)
                t.wait_ge(sems["gw"], 3)
                twait(t, "wg")
                for tc in range(NT):
                    bank, off = (H1, 0) if tc % 2 == 0 else (H1, 512)
                    if tc >= 2:
                        t.wait_ge(sems["gsig"], tc - 1)
                    mm = None
                    for k in range(8):
                        mm = t.matmul(bank[:, off:off + 8],
                                      h2fm[:, k * 2048 + tc * 128: k * 2048 + (tc + 1) * 128],
                                      wg_sb[:, k * 8:(k + 1) * 8], start=(k == 0), stop=(k == 7))
                    mm.then_inc(sems["gmm"], 1)
            V["gmm"] += NT

            @block.scalar
            def _(s):
                for tc in range(NT):
                    bank, off = (H1, 0) if tc % 2 == 0 else (H1, 512)
                    s.wait_ge(sems["gmm"], tc + 1)
                    s.activation(gates[:, tc * 8:(tc + 1) * 8], bank[:, off:off + 8],
                                 AF.Sigmoid).then_inc(sems["gsig"], 1)
            V["gsig"] += NT

            @block.vector
            def _(v):
                v.wait_ge(sems["gsig"], NT)
                for tc in range(NT):
                    v.max(top8[:, tc * 8:(tc + 1) * 8], gates[:, tc * 8:(tc + 1) * 8])
                v.drain()
                for tc in range(NT):
                    v.tensor_add(gsum[:, tc:tc + 1], top8[:, tc * 8:tc * 8 + 1],
                                 top8[:, tc * 8 + 1:tc * 8 + 2])
                v.drain()
                for tc in range(NT):
                    v.tensor_scalar(gsum[:, tc:tc + 1], gsum[:, tc:tc + 1], 3.0, None, ALU.mult)
                v.drain()
                for tc in range(NT):
                    v.reciprocal(grcp[:, tc:tc + 1], gsum[:, tc:tc + 1])
                for tc in range(NT):
                    v.tensor_scalar(gm1[:, tc * 8:(tc + 1) * 8], gates[:, tc * 8:(tc + 1) * 8],
                                    top8[:, tc * 8:tc * 8 + 1], None, ALU.is_equal)
                for tc in range(NT):
                    v.tensor_scalar(gm2[:, tc * 8:(tc + 1) * 8], gates[:, tc * 8:(tc + 1) * 8],
                                    top8[:, tc * 8 + 1:tc * 8 + 2], None, ALU.is_equal)
                v.drain()
                for tc in range(NT):
                    v.tensor_max(gm1[:, tc * 8:(tc + 1) * 8], gm1[:, tc * 8:(tc + 1) * 8],
                                 gm2[:, tc * 8:(tc + 1) * 8])
                v.drain()
                for tc in range(NT):
                    v.scalar_tensor_tensor(w_sb[:, tc * 8:(tc + 1) * 8],
                                           gates[:, tc * 8:(tc + 1) * 8], grcp[:, tc:tc + 1],
                                           gm1[:, tc * 8:(tc + 1) * 8], ALU.mult, ALU.mult)
                v.drain().then_inc(sems["gw"], 1)
            V["gw"] += 1
            GWW = V["gw"]

            if "wdbg" in dbg:
                @block.sync
                def _(sync):
                    sync.wait_ge(sems["gw"], GWW)
                    tok["d_wdbg"] = pool_s.issue(sync, dbg["wdbg"], w_sb[:])

            steps = [(e, fh) for e in range(9) for fh in range(2)]

            @block.gpsimd
            def _(g):
                for si, (e, fh) in enumerate(steps):
                    if si >= 1:
                        g.wait_ge(sems["fA"], 64 * si)
                    for nm, w9, buf in (("we1", we1_, we1h), ("we3", we3_, we3h)):
                        src = w9[e].rearrange("(k p) (h f) -> h p k f", p=128, f=1024, h=2)[fh]
                        tok[nm, si] = pool_g.issue(g, buf[:].rearrange("p (k f) -> p k f", f=1024), src)
                    if si >= 1:
                        g.wait_ge(sems["fB"], 32 * si)
                    tok["we2", si] = pool_g.issue(
                        g, we2h[:].rearrange("p (fc d) -> p fc d", d=1024),
                        we2_[e].rearrange("(h fc p) d -> h p fc d", p=128, fc=8)[fh])

            @block.tensor
            def _(t):
                t.wait_ge(sems["gsig"], NT)
                for si, (e, fh) in enumerate(steps):
                    twait(t, ("we1", si))
                    twait(t, ("we3", si))
                    if si >= 1:
                        t.wait_ge(sems["fEv"], si * 16)
                    for fc in range(8):
                        if si > 0 or fc > 0:
                            t.wait_ge(sems["fSl"], si * 8 + fc)
                        for nb in range(4):
                            mm = None
                            for k in range(8):
                                mm = t.matmul(H1[:, nb * 512:(nb + 1) * 512],
                                              we1h[:, k * 1024 + fc * 128: k * 1024 + (fc + 1) * 128],
                                              h2fm[:, k * 2048 + nb * 512: k * 2048 + (nb + 1) * 512],
                                              start=(k == 0), stop=(k == 7))
                            mm.then_inc(sems["fA"], 1)
                        if si > 0 or fc > 0:
                            t.wait_ge(sems["fG"], si * 8 + fc)
                        for nb in range(4):
                            mm = None
                            for k in range(8):
                                mm = t.matmul(H3[:, nb * 512:(nb + 1) * 512],
                                              we3h[:, k * 1024 + fc * 128: k * 1024 + (fc + 1) * 128],
                                              h2fm[:, k * 2048 + nb * 512: k * 2048 + (nb + 1) * 512],
                                              start=(k == 0), stop=(k == 7))
                            mm.then_inc(sems["fA"], 1)
                    t.wait_ge(sems["fG"], (si + 1) * 8)
                    t.wait_ge(sems["wsc"], si + 1)
                    for tc in range(NT):
                        off = 0 if tc % 2 == 0 else 1024
                        if tc >= 2:
                            t.wait_ge(sems["fEv"], si * 16 + tc - 1)
                        for nb in range(2):
                            mm = None
                            for fc in range(8):
                                mm = t.matmul(H1[:, off + nb * 512: off + (nb + 1) * 512],
                                              Gbuf[:, fc * 2048 + tc * 128: fc * 2048 + (tc + 1) * 128],
                                              we2h[:, fc * 1024 + nb * 512: fc * 1024 + (nb + 1) * 512],
                                              start=(fc == 0), stop=(fc == 7))
                            mm.then_inc(sems["fB"], 1)
            V["fA"] += len(steps) * 64
            V["fB"] += len(steps) * 32

            with nc.sbuf_tensor("slbuf", [128, 2 * 2048], BF16) as slbuf:
                @block.scalar
                def _(s):
                    for si, (e, fh) in enumerate(steps):
                        for fc in range(8):
                            s.wait_ge(sems["fA"], si * 64 + fc * 8 + 4)
                            if si * 8 + fc >= 2:
                                s.wait_ge(sems["fG"], si * 8 + fc - 1)
                            s.activation(slbuf[:, (fc % 2) * 2048:(fc % 2 + 1) * 2048],
                                         H1[:, 0:2048], AF.Silu).then_inc(sems["fSl"], 1)
                V["fSl"] += len(steps) * 8

                @block.vector
                def _(v):
                    v.wait_ge(sems["gw"], GW0)
                    for si, (e, fh) in enumerate(steps):
                        twait(v, ("we2", si))
                        for fc in range(8):
                            if e == 8:
                                v.scalar_tensor_tensor(we2h[:, fc * 1024:(fc + 1) * 1024],
                                                       we2h[:, fc * 1024:(fc + 1) * 1024],
                                                       cst[:, 4:5], gb_bc[:],
                                                       ALU.mult, ALU.mult)
                            else:
                                v.tensor_mul(we2h[:, fc * 1024:(fc + 1) * 1024],
                                             we2h[:, fc * 1024:(fc + 1) * 1024], gb_bc[:])
                        v.drain().then_inc(sems["wsc"], 1)
                        for fc in range(8):
                            v.wait_ge(sems["fSl"], si * 8 + fc + 1)
                            v.wait_ge(sems["fA"], si * 64 + fc * 8 + 8)
                            v.tensor_mul(Gbuf[:, fc * 2048:(fc + 1) * 2048],
                                         slbuf[:, (fc % 2) * 2048:(fc % 2 + 1) * 2048],
                                         H3[:, 0:2048]).then_inc(sems["fG"], 1)
                        for tc in range(NT):
                            off = 0 if tc % 2 == 0 else 1024
                            if si == 0 and tc == 0 and "xmid" in dbg:
                                twait(v, "d_xmid")
                            v.wait_ge(sems["fB"], si * 32 + 2 * (tc + 1))
                            if e == 8:
                                v.tensor_add(x_sb[:, tc * D:(tc + 1) * D],
                                             x_sb[:, tc * D:(tc + 1) * D],
                                             H1[:, off:off + 1024]).then_inc(sems["fEv"], 1)
                            else:
                                v.scalar_tensor_tensor(x_sb[:, tc * D:(tc + 1) * D],
                                                       H1[:, off:off + 1024],
                                                       w_sb[:, tc * 8 + e:tc * 8 + e + 1],
                                                       x_sb[:, tc * D:(tc + 1) * D],
                                                       ALU.mult, ALU.add).then_inc(sems["fEv"], 1)
                V["fG"] += len(steps) * 8
                V["fEv"] += len(steps) * 16
                V["wsc"] += len(steps)

        @block.sync
        def _(sync):
            sync.wait_ge(sems["fEv"], V["fEv"])
            sync.dma_start(out=out_.rearrange("(n p) f -> p n f", p=128),
                           in_=x_sb[:].rearrange("p (n f) -> p n f", f=D)).then_inc(sems["sto"], 16)
            sync.wait_ge(sems["sto"], 16)
            for key in ("d_mod48", "d_xmid", "d_wdbg"):
                if key in tok:
                    twait(sync, key)
        V["sto"] += 16

        fscope.close()
    return nc, dbg


def _prep_inputs(inputs, core):
    x = np.asarray(inputs["L0_attn_x"], dtype=np.float32).reshape(-1, D)
    c = np.asarray(inputs["L0_c"], dtype=np.float32)
    bqkv = np.asarray(inputs["L0_attn_bqkv"], dtype=np.float32)
    bqkvv = np.ascontiguousarray(bqkv.reshape(16, 192)[:, 128:].reshape(1, D))
    we1 = np.concatenate([np.asarray(inputs["L0_ffn_We1"], np.float32),
                          np.asarray(inputs["L0_ffn_Ws1"], np.float32)[None]], axis=0)
    we3 = np.concatenate([np.asarray(inputs["L0_ffn_We3"], np.float32),
                          np.asarray(inputs["L0_ffn_Ws3"], np.float32)[None]], axis=0)
    we2 = np.concatenate([np.asarray(inputs["L0_ffn_We2"], np.float32),
                          np.asarray(inputs["L0_ffn_Ws2"], np.float32)[None]], axis=0)
    return {
        "x": np.ascontiguousarray(x[core * T:(core + 1) * T]),
        "wqkv": np.ascontiguousarray(np.asarray(inputs["L0_attn_Wqkv"], np.float32)),
        "bqkv": np.ascontiguousarray(bqkv.reshape(24, 128).T),
        "bqkvv": bqkvv,
        "wo": np.ascontiguousarray(np.asarray(inputs["L0_attn_Wo"], np.float32)),
        "bo": np.ascontiguousarray(np.asarray(inputs["L0_attn_bo"], np.float32).reshape(8, 128).T),
        "wg": np.ascontiguousarray(np.asarray(inputs["L0_ffn_Wg"], np.float32).reshape(8, 128, 8).transpose(1, 0, 2).reshape(128, 64)),
        "we1": np.ascontiguousarray(we1),
        "we3": np.ascontiguousarray(we3),
        "we2": np.ascontiguousarray(we2),
        "cvec": np.ascontiguousarray(c[core // 2].reshape(8, 128).T),
        "wmod": np.ascontiguousarray(np.asarray(inputs["L0_wmod"], np.float32)),
        "ident": np.eye(128, dtype=ml_dtypes.bfloat16),
        "identf": np.eye(128, dtype=np.float32),
    }


def kernel(**inputs):
    nc, _ = build()
    in_maps = [_prep_inputs(inputs, core) for core in range(8)]
    res = run_bass_kernel_spmd(nc, in_maps, core_ids=list(range(8)))
    out = np.concatenate([np.asarray(r["out"]) for r in res.results], axis=0)
    return out.reshape(4, 4096, D).astype(np.float32)


# version 08
# speedup vs baseline: 1.0302x; 1.0302x over previous
import contextlib
import numpy as np
import ml_dtypes
import concourse.bass as bass
import concourse.mybir as mybir
from concourse.bass_utils import run_bass_kernel_spmd

F32 = mybir.dt.float32
BF16 = mybir.dt.bfloat16
AF = mybir.ActivationFunctionType
ALU = mybir.AluOpType

T = 2048
D = 1024
NT = 16
NCH = 8
EXPC = 8.0
INTERLEAVE = True
SROW_ACT = True
PAIR_EXP = True
SEXP_CH = 128 if PAIR_EXP else 256
DUMMY_TAPS = False


def attn_sched():
    if INTERLEAVE:
        ev = []
        for sl in range(10):
            if sl < 8:
                ev.append(("b", sl))
            if sl == 1:
                ev.append(("m",))
            if sl >= 2:
                ev.append(("s", sl - 2))
        return ev
    return [("b", i) for i in range(8)] + [("m",)] + [("s", i) for i in range(8)]

SEMS = ["mod", "slc", "lns", "ln", "tr", "trev", "qmm", "qev", "s2m", "s2r", "s2f", "mtr", "mng", "pbs0", "pbs1",
        "vmm", "vev", "smm", "sexp", "spv", "srow", "sbc", "snrm", "womm",
        "oev", "etr", "eadd", "ln2s", "ln2", "tr2", "trev2", "gmm", "gsig",
        "gw", "fA", "fSl", "fG", "fB", "fEv", "wsc", "bar", "sto", "dbgs"]


class DPool:

    def __init__(self, nc, ctx, prefix, n):
        self.sems = [ctx.enter_context(nc.semaphore(prefix + str(i))) for i in range(n)]
        self.vals = [0] * n
        self.rr = 0

    def issue(self, eng, out, in_):
        k = self.rr
        self.rr = (self.rr + 1) % len(self.sems)
        if self.vals[k] > 0:
            eng.wait_ge(self.sems[k], self.vals[k])
        eng.dma_start(out=out, in_=in_).then_inc(self.sems[k], 16)
        self.vals[k] += 16
        return (self.sems[k], self.vals[k])


def qev_pref(ft):
    pat = [2, 1, 1]
    return sum(pat[f % 3] for f in range(ft))


QEV_CH = qev_pref(24)


def build(debug=()):
    nc = bass.Bass("TRN2", target_bir_lowering=False)

    x_ = nc.dram_tensor("x", [T, D], F32, kind="ExternalInput").ap()
    wqkv_ = nc.dram_tensor("wqkv", [3 * D, D], F32, kind="ExternalInput").ap()
    bqkv_ = nc.dram_tensor("bqkv", [128, 24], F32, kind="ExternalInput").ap()
    bqkvv_ = nc.dram_tensor("bqkvv", [1, D], F32, kind="ExternalInput").ap()
    wo_ = nc.dram_tensor("wo", [D, D], F32, kind="ExternalInput").ap()
    bo_ = nc.dram_tensor("bo", [128, 8], F32, kind="ExternalInput").ap()
    wg_ = nc.dram_tensor("wg", [128, 64], F32, kind="ExternalInput").ap()
    we1_ = nc.dram_tensor("we1", [9, D, 2048], F32, kind="ExternalInput").ap()
    we3_ = nc.dram_tensor("we3", [9, D, 2048], F32, kind="ExternalInput").ap()
    we2_ = nc.dram_tensor("we2", [9, 2048, D], F32, kind="ExternalInput").ap()
    cvec_ = nc.dram_tensor("cvec", [128, 8], F32, kind="ExternalInput").ap()
    wmod_ = nc.dram_tensor("wmod", [D, 6 * D], F32, kind="ExternalInput").ap()
    ident_ = nc.dram_tensor("ident", [128, 128], BF16, kind="ExternalInput").ap()
    identf_ = nc.dram_tensor("identf", [128, 128], F32, kind="ExternalInput").ap()
    out_ = nc.dram_tensor("out", [T, D], F32, kind="ExternalOutput").ap()
    dbg = {}
    for name, shape in debug:
        dbg[name] = nc.dram_tensor("dbg_" + name, list(shape), F32, kind="ExternalOutput").ap()
    if DUMMY_TAPS:
        for name, shape in [("mod48", (128, 48)), ("xmid", (T, D)), ("wdbg", (128, 128))]:
            if name not in dbg:
                dbg[name] = nc.dram_tensor("dbg_" + name, list(shape), F32, kind="ExternalOutput").ap()

    ctx = contextlib.ExitStack()
    with ctx:
        E = ctx.enter_context

        x_sb = E(nc.sbuf_tensor("x_sb", [128, NT * D], F32))
        ident = E(nc.sbuf_tensor("identb_sb", [128, 128], BF16))
        identf = E(nc.sbuf_tensor("identf_sb", [128, 128], F32))
        cst = E(nc.sbuf_tensor("cst", [128, 8], F32))
        ones_bf = E(nc.sbuf_tensor("ones_bf", [1, 128], BF16))
        ones_f32 = E(nc.sbuf_tensor("ones_f32", [128, 1], F32))
        mod_sb = E(nc.sbuf_tensor("mod_sb", [128, 48], F32))
        ca1 = E(nc.sbuf_tensor("ca1", [128, 8], F32))
        cb1 = E(nc.sbuf_tensor("cb1", [128, 8], F32))
        ga_v = E(nc.sbuf_tensor("ga_v", [128, 8], F32))
        gb_v = E(nc.sbuf_tensor("gb_v", [128, 8], F32))
        gabo_v = E(nc.sbuf_tensor("gabo_v", [128, 8], F32))
        bo_fm = E(nc.sbuf_tensor("bo_fm", [128, 8], F32))
        t8a = E(nc.sbuf_tensor("t8a", [128, 8], F32))
        t8b = E(nc.sbuf_tensor("t8b", [128, 8], F32))
        c_fm = E(nc.sbuf_tensor("c_fm", [128, 8], F32))
        silc = E(nc.sbuf_tensor("silc", [128, 8], F32))
        bqkv_sb = E(nc.sbuf_tensor("bqkv_sb", [128, 24], F32))
        qmask = E(nc.sbuf_tensor("qmask", [128, 24], F32))
        bq_eff = E(nc.sbuf_tensor("bq_eff", [128, 24], F32))
        bqv_row = E(nc.sbuf_tensor("bqv_row", [1, D], F32))
        bqv_bf = E(nc.sbuf_tensor("bqv_bf", [1, D], BF16))
        wg_sb = E(nc.sbuf_tensor("wg_sb", [128, 64], BF16))
        st6 = E(nc.sbuf_tensor("st6", [128, NT * 12], F32))
        mv = E(nc.sbuf_tensor("mv", [128, NT * 2], F32))
        lnv = E(nc.sbuf_tensor("lnv", [128, NT], F32))
        r_all = E(nc.sbuf_tensor("r_all", [128, NT], F32))

        sems = {name: E(nc.semaphore("s_" + name)) for name in SEMS}
        V = {name: 0 for name in SEMS}
        pool_s = DPool(nc, ctx, "ps_", 6)
        pool_g = DPool(nc, ctx, "pg_", 6)
        tok = {}

        def twait(eng, key):
            sem, val = tok[key]
            eng.wait_ge(sem, val)

        block = E(nc.Block())
        nbar = [0]

        def bar_all():
            nbar[0] += 1
            k = nbar[0]
            for sec in (block.tensor, block.scalar, block.vector):
                def mk(eng, _k=k):
                    eng.drain().then_inc(sems["bar"], 1)
                    eng.wait_ge(sems["bar"], 5 * _k)
                sec(mk)
            for sec in (block.gpsimd, block.sync):
                def mk2(eng, _k=k):
                    eng.nop().then_inc(sems["bar"], 1)
                    eng.wait_ge(sems["bar"], 5 * _k)
                sec(mk2)
            V["bar"] = 5 * k

        @block.sync
        def _(sync):
            tok["x"] = pool_s.issue(sync, x_sb[:].rearrange("p (n f) -> p n f", f=D),
                                    x_.rearrange("(n p) f -> p n f", p=128))
            tok["ident"] = pool_s.issue(sync, ident[:], ident_)
            tok["identf"] = pool_s.issue(sync, identf[:], identf_)
            tok["cfm"] = pool_s.issue(sync, c_fm[:], cvec_)
            tok["bqkv"] = pool_s.issue(sync, bqkv_sb[:], bqkv_)
            tok["bqvrow"] = pool_s.issue(sync, bqv_row[:], bqkvv_)
            tok["bofm"] = pool_s.issue(sync, bo_fm[:], bo_)

        @block.gpsimd
        def _(g):
            tok["wg"] = pool_g.issue(g, wg_sb[:], wg_)

        @block.vector
        def _(v):
            v.memset(cst[:, 0:1], -1.0)
            v.memset(cst[:, 1:2], 0.125)
            v.memset(cst[:, 2:3], 1e-5)
            v.memset(cst[:, 3:4], -0.5)
            v.memset(cst[:, 4:5], 1.0 / 3.0)
            v.memset(cst[:, 5:6], 1.0)
            v.memset(cst[:, 6:7], 0.25)
            v.memset(ones_bf[:], 1.0)
            v.memset(ones_f32[:], 1.0)
            v.memset(qmask[0:64, 0:24:3], 0.125)
            v.memset(qmask[64:128, 0:24:3], 1.0)
            v.memset(qmask[0:64, 1:24:3], 1.0)
            v.memset(qmask[64:128, 1:24:3], 0.125)
            v.memset(qmask[:, 2:24:3], 1.0)
            twait(v, "bqkv")
            twait(v, "bqvrow")
            v.drain()
            v.tensor_mul(bq_eff[:], bqkv_sb[:], qmask[:])
            v.tensor_copy(bqv_bf[:], bqv_row[:])
            v.drain().then_inc(sems["mod"], 1)
        V["mod"] += 1

        @block.vector
        def _(v):
            twait(v, "x")
            for i in range(NT):
                v.bn_stats(st6[:, i * 12:i * 12 + 6], x_sb[:, i * D:i * D + 512])
                v.bn_stats(st6[:, i * 12 + 6:i * 12 + 12], x_sb[:, i * D + 512:(i + 1) * D])
            v.drain()
            for i in range(NT):
                v.bn_aggr(mv[:, i * 2:(i + 1) * 2], st6[:, i * 12:(i + 1) * 12])
            v.drain().then_inc(sems["lns"], 1)
        V["lns"] += 1

        @block.scalar
        def _(s):
            s.wait_ge(sems["lns"], 1)
            s.wait_ge(sems["mod"], 1)
            var_view = mv[:].rearrange("p (n two) -> p n two", two=2)[:, :, 1:2]
            s.activation(lnv[:], var_view, AF.Ln, bias=cst[:, 2:3])
            s.drain()
            s.activation(r_all[:], lnv[:], AF.Exp, scale=cst[:, 3:4]).then_inc(sems["lns"], 1)
        V["lns"] += 1

        wscope = contextlib.ExitStack()
        wqkvT = wscope.enter_context(nc.sbuf_tensor("wqkvT", [128, 8 * 3072], BF16))
        woT = wscope.enter_context(nc.sbuf_tensor("woT", [128, 8 * 1024], BF16))

        with nc.sbuf_tensor("wnat", [128, 8 * 1024], BF16) as wnat, \
             nc.psum_tensor("trp0", [128, 1024], BF16) as trp0, \
             nc.psum_tensor("trp1", [128, 1024], BF16) as trp1:

            @block.gpsimd
            def _(g):
                for grp in range(4):
                    if grp >= 1:
                        g.wait_ge(sems["trev"], (grp - 1) * 8 + 8)
                    if grp < 3:
                        src = wqkv_.rearrange("(g j p) d -> g p j d", p=128, j=8)[grp]
                    else:
                        src = wo_.rearrange("(j p) d -> p j d", p=128)
                    tok["wnat", grp] = pool_g.issue(g, wnat[:].rearrange("p (j d) -> p j d", d=D), src)

            @block.tensor
            def _(t):
                twait(t, "ident")
                for grp in range(4):
                    twait(t, ("wnat", grp))
                    for j in range(8):
                        bank = trp0 if j % 2 == 0 else trp1
                        idx = grp * 8 + j
                        if idx >= 2:
                            t.wait_ge(sems["trev"], idx - 1)
                        for k in range(8):
                            t.transpose(bank[:, k * 128:(k + 1) * 128],
                                        wnat[:, j * 1024 + k * 128: j * 1024 + (k + 1) * 128],
                                        ident[:]).then_inc(sems["tr"], 1)
            V["tr"] += 256
            TRW = V["tr"]

            @block.scalar
            def _(s):
                for grp in range(4):
                    for j in range(8):
                        bank = trp0 if j % 2 == 0 else trp1
                        idx = grp * 8 + j
                        s.wait_ge(sems["tr"], 8 * idx + 8)
                        if grp < 3:
                            ft = grp * 8 + j
                            dst = wqkvT[:].rearrange("p (k f) -> p k f", f=3072)[:, :, ft * 128:(ft + 1) * 128]
                        else:
                            dst = woT[:].rearrange("p (k f) -> p k f", f=1024)[:, :, j * 128:(j + 1) * 128]
                        s.activation(dst, bank[:].rearrange("p (k f) -> p k f", f=128),
                                     AF.Copy).then_inc(sems["trev"], 1)
            V["trev"] += 32
            TRV0 = V["trev"]

        with nc.sbuf_tensor("wmbuf", [128, 2 * 3072], F32) as wmbuf, \
             nc.sbuf_tensor("macc", [128, 6144], F32) as macc, \
             nc.psum_tensor("mred", [128, 64], F32) as mred:

            @block.scalar
            def _(s):
                twait(s, "cfm")
                s.activation(silc[:], c_fm[:], AF.Silu).then_inc(sems["slc"], 1)
            V["slc"] += 1

            MODB = V["mod"] + 1

            @block.sync
            def _(sync):
                sync.wait_ge(sems["tr"], TRW)
                for h in range(16):
                    if h >= 2:
                        sync.wait_ge(sems["mod"], MODB + 4 * (h - 1))
                    k, hh = divmod(h, 2)
                    tok["wm", h] = pool_s.issue(sync, wmbuf[:, (h % 2) * 3072:(h % 2 + 1) * 3072],
                                                wmod_[k * 128:(k + 1) * 128, hh * 3072:(hh + 1) * 3072])

            @block.vector
            def _(v):
                v.memset(macc[:], 0.0).then_inc(sems["mod"], 1)
                v.wait_ge(sems["slc"], 1)
                v.wait_ge(sems["mod"], MODB)
                for h in range(16):
                    k, hh = divmod(h, 2)
                    twait(v, ("wm", h))
                    for q in range(4):
                        dsl = slice(hh * 3072 + q * 768, hh * 3072 + (q + 1) * 768)
                        ssl = slice((h % 2) * 3072 + q * 768, (h % 2) * 3072 + (q + 1) * 768)
                        v.scalar_tensor_tensor(macc[:, dsl], wmbuf[:, ssl], silc[:, k:k + 1],
                                               macc[:, dsl], ALU.mult, ALU.add).then_inc(sems["mod"], 1)
            V["mod"] += 1 + 64

            @block.tensor
            def _(t):
                t.wait_ge(sems["mod"], V["mod"])
                t.wait_ge(sems["trev"], TRV0)
                for col in range(48):
                    t.matmul(mred[:, col:col + 1], macc[:, col * 128:(col + 1) * 128],
                             ones_f32[:], start=True, stop=True)
                t.drain().then_inc(sems["mod"], 1)
            V["mod"] += 1

            @block.vector
            def _(v):
                v.wait_ge(sems["mod"], V["mod"])
                v.tensor_copy(mod_sb[:], mred[:, 0:48])
                v.drain()
                ca = mod_sb[:, 8:16]
                ga = mod_sb[:, 16:24]
                cb = mod_sb[:, 32:40]
                gb = mod_sb[:, 40:48]
                v.tensor_scalar(t8a[:], ga, -1.0, None, ALU.mult)
                v.tensor_scalar(t8b[:], gb, -1.0, None, ALU.mult)
                v.tensor_scalar(ca1[:], ca, 1.0, None, ALU.add)
                v.tensor_scalar(cb1[:], cb, 1.0, None, ALU.add)
                v.drain()
                v.tensor_max(t8a[:], ga, t8a[:])
                v.tensor_max(t8b[:], gb, t8b[:])
                v.tensor_scalar(ga_v[:], ga, 0.0, None, ALU.is_ge)
                v.tensor_scalar(gb_v[:], gb, 0.0, None, ALU.is_ge)
                v.drain()
                v.tensor_scalar(t8a[:], t8a[:], 0.25, None, ALU.max)
                v.tensor_scalar(t8b[:], t8b[:], 0.25, None, ALU.max)
                v.tensor_scalar(ga_v[:], ga_v[:], 2.0, -1.0, ALU.mult, ALU.add)
                v.tensor_scalar(gb_v[:], gb_v[:], 2.0, -1.0, ALU.mult, ALU.add)
                v.drain()
                v.tensor_mul(ga_v[:], ga_v[:], t8a[:])
                v.tensor_mul(gb_v[:], gb_v[:], t8b[:])
                v.drain()
                twait(v, "bofm")
                v.tensor_mul(gabo_v[:], ga_v[:], bo_fm[:])
                v.drain().then_inc(sems["mod"], 1)
            V["mod"] += 1
        MOD_DONE = V["mod"]

        bar_all()

        if "mod48" in dbg:
            @block.sync
            def _(sync):
                tok["d_mod48"] = pool_s.issue(sync, dbg["mod48"], mod_sb[:])

        ascope = contextlib.ExitStack()
        hn1 = ascope.enter_context(nc.sbuf_tensor("hn1", [128, 4 * D], F32))
        h1c = ascope.enter_context(nc.sbuf_tensor("h1c", [128, 8 * 256], BF16))
        Qb = ascope.enter_context(nc.sbuf_tensor("Qb", [65, 4096], BF16))
        Kb = ascope.enter_context(nc.sbuf_tensor("Kb", [65, 4096], BF16))
        MP4 = ascope.enter_context(nc.sbuf_tensor("MP4", [128, 128], F32))
        M_part = ascope.enter_context(nc.sbuf_tensor("M_part", [128, 32], F32))
        Vtm = ascope.enter_context(nc.sbuf_tensor("Vtm", [128, 32 * 65], BF16))
        Eb = ascope.enter_context(nc.sbuf_tensor("Eb", [128, 4 * 512], BF16))
        att = ascope.enter_context(nc.sbuf_tensor("att", [128, 8 * 256], BF16))
        t1c = ascope.enter_context(nc.sbuf_tensor("t1c", [128, 8 * 256], F32))
        rsb = ascope.enter_context(nc.sbuf_tensor("rsb", [1, 2 * 512], F32))
        Pbs = ascope.enter_context(nc.sbuf_tensor("Pbs", [64, 2 * 512], F32))
        PS = ascope.enter_context(nc.psum_tensor("PS", [128, 4096], F32))
        R = [PS[:, i * 512:(i + 1) * 512] for i in range(8)]

        @block.vector
        def _(v):
            v.memset(Vtm[:], 1.0)
            v.memset(Kb[64:65, :], 1.0)
            v.drain().then_inc(sems["vev"], 1)
        V["vev"] += 1
        VEV0 = V["vev"]


        @block.tensor
        def _(t):
            t.wait_ge(sems["bar"], V["bar"])
            t.wait_ge(sems["mod"], MOD_DONE)
            twait(t, "identf")
            for c in range(NCH):
                t.wait_ge(sems["ln"], c // 2 + 1)
                if c >= 1:
                    t.wait_ge(sems["s2r"], 128 * c)
                for u in range(2):
                    for k in range(8):
                        idx = u * 8 + k
                        if idx >= 4:
                            t.wait_ge(sems["trev"], TRV0 + 16 * c + idx - 3)
                        ti = (2 * c + u) % 4
                        t.transpose(R[idx % 4][:, 0:128],
                                    hn1[:, ti * D + k * 128: ti * D + (k + 1) * 128],
                                    identf[:]).then_inc(sems["tr"], 1)
                V["tr"] += 16
                t.wait_ge(sems["trev"], TRV0 + 16 * (c + 1))
                if c >= 1:
                    t.wait_ge(sems["eadd"], 4 * c)
                for ft in range(24):
                    bank = R[4 + ft % 2]
                    if ft >= 2:
                        t.wait_ge(sems["qev"], QEV_CH * c + qev_pref(ft - 1))
                    mm = None
                    for k in range(8):
                        mm = t.matmul(bank[:, 0:256],
                                      wqkvT[:, k * 3072 + ft * 128: k * 3072 + (ft + 1) * 128],
                                      h1c[:, k * 256:(k + 1) * 256],
                                      start=(k == 0), stop=(k == 7))
                    mm.then_inc(sems["qmm"], 1)
                V["qmm"] += 24
                for vb in range(4):
                    u, half = vb // 2, vb % 2
                    bank = R[6 + vb % 2]
                    if vb >= 2:
                        t.wait_ge(sems["vev"], VEV0 + 4 * c + vb - 1)
                    elif c > 0:
                        t.wait_ge(sems["snrm"], 16 * c)
                    wq4 = wqkvT[:].rearrange("p (k G seg) -> p k G seg", G=16, seg=192)
                    for k in range(8):
                        t.matmul(bank[:, 0:512],
                                 h1c[:, k * 256 + u * 128: k * 256 + u * 128 + 128],
                                 wq4[:, k, half * 8:(half + 1) * 8, 128:192],
                                 start=(k == 0), stop=False)
                    t.matmul(bank[:, 0:512], ones_bf[:],
                             bqv_bf[0:1, half * 512:(half + 1) * 512],
                             start=False, stop=True).then_inc(sems["vmm"], 1)
                V["vmm"] += 4
                t.wait_ge(sems["qev"], QEV_CH * (c + 1))
                t.wait_ge(sems["vev"], VEV0 + 4 * (c + 1))
                for ev in attn_sched():
                    if ev[0] == "b":
                        sl = ev[1]
                        for qt in range(4 * sl, 4 * sl + 4):
                            for pc in range(4):
                                b = 4 * qt + pc
                                if b >= 2:
                                    t.wait_ge(sems["s2r"], 128 * c + b - 1)
                                for nb in range(2):
                                    t.matmul(PS[:, (b % 2) * 1024 + nb * 512:(b % 2) * 1024 + (nb + 1) * 512],
                                             Qb[0:64, qt * 128:(qt + 1) * 128],
                                             Kb[0:64, pc * 1024 + nb * 512: pc * 1024 + (nb + 1) * 512],
                                             start=True, stop=True).then_inc(sems["s2m"], 1)
                    if ev[0] == "m":
                        for i in range(4):
                            t.wait_ge(sems["s2f"], 32 * c + i + 1)
                            t.transpose(R[4 + i][0:1, 0:128],
                                        M_part[:, i:i + 1], identf[:]).then_inc(sems["mtr"], 1)
                    if ev[0] == "s":
                        sq = ev[1]
                        g = 8 * c + sq
                        Ob = R[6 + sq % 2]
                        for jp in range(16):
                            P = 16 * sq + jp
                            if sq < 7 and 13 <= jp <= 15:
                                t.wait_ge(sems["mng"], 32 * c + 4 * (sq + 1) + (jp - 13) + 1)
                            if jp == 0:
                                t.wait_ge(sems["mng"], 32 * c + 4 * (sq + 1))
                            for h in range(2):
                                jt = 2 * jp + h
                                TK = 32 * sq + jt
                                if PAIR_EXP:
                                    if h == 0 and P >= 1:
                                        t.wait_ge(sems["sexp"], 128 * c + P)
                                elif TK >= 2:
                                    t.wait_ge(sems["sexp"], 256 * c + TK - 1)
                                t.matmul(R[4 + h][:, 0:512], Kb[:, jt * 128:(jt + 1) * 128],
                                         Qb[:, sq * 512:(sq + 1) * 512],
                                         start=True, stop=True).then_inc(sems["smm"], 1)
                            if PAIR_EXP:
                                t.wait_ge(sems["sexp"], 128 * c + P + 1)
                            for h in range(2):
                                jt = 2 * jp + h
                                TK = 32 * sq + jt
                                if not PAIR_EXP:
                                    t.wait_ge(sems["sexp"], 256 * c + TK + 1)
                                if jt == 0 and g >= 2:
                                    t.wait_ge(sems["snrm"], 2 * (g - 1))
                                t.matmul(Ob[0:65, 0:512], Vtm[:, jt * 65: jt * 65 + 65],
                                         Eb[:, (jt % 4) * 512:(jt % 4) * 512 + 512],
                                         start=(jt == 0), stop=(jt == 31)).then_inc(sems["spv"], 1)
                            if sq < 7 and 12 <= jp <= 15:
                                qt2 = 4 * (sq + 1) + (jp - 12)
                                t.wait_ge(sems["s2f"], 32 * c + qt2 + 1)
                                t.transpose(R[4][0:1, 0:128], M_part[:, qt2:qt2 + 1],
                                            identf[:]).then_inc(sems["mtr"], 1)
                V["s2m"] += 256
                V["smm"] += 256
                V["spv"] += 256
                V["mtr"] += 32
                t.wait_ge(sems["snrm"], 16 * (c + 1))
                t.wait_ge(sems["sexp"], SEXP_CH * (c + 1))
                for j in range(8):
                    bank = R[4 + j % 4]
                    if j >= 4:
                        t.wait_ge(sems["oev"], 8 * c + j - 3)
                    mm = None
                    for ki in range(8):
                        mm = t.matmul(bank[:, 0:256],
                                      woT[:, ki * 1024 + j * 128: ki * 1024 + (j + 1) * 128],
                                      att[:, ki * 256:(ki + 1) * 256],
                                      start=(ki == 0), stop=(ki == 7))
                    mm.then_inc(sems["womm"], 1)
                V["womm"] += 8
                t.wait_ge(sems["oev"], 8 * (c + 1))
                for grp in range(4):
                    uh, jh = grp // 2, grp % 2
                    bank = R[4 + grp % 2]
                    if grp >= 2:
                        t.wait_ge(sems["eadd"], 4 * c + grp - 1)
                    for l in range(4):
                        j = jh * 4 + l
                        t.transpose(bank[:, l * 128:(l + 1) * 128],
                                    t1c[:, j * 256 + uh * 128: j * 256 + uh * 128 + 128],
                                    identf[:]).then_inc(sems["etr"], 1)
                V["etr"] += 16

        @block.scalar
        def _(s):
            s.wait_ge(sems["bar"], V["bar"])
            s.wait_ge(sems["mod"], MOD_DONE)
            for c in range(NCH):
                for u in range(2):
                    for k in range(8):
                        idx = u * 8 + k
                        s.wait_ge(sems["tr"], TRW + 16 * c + idx + 1)
                        s.activation(h1c[:, k * 256 + u * 128: k * 256 + u * 128 + 128],
                                     R[idx % 4][:, 0:128], AF.Identity,
                                     scale=ca1[:, k:k + 1],
                                     bias=mod_sb[:, k:k + 1]).then_inc(sems["trev"], 1)
                for i in range(4):
                    s.wait_ge(sems["mtr"], 32 * c + i + 1)
                    s.activation(Qb[64:65, i * 128:(i + 1) * 128],
                                 R[4 + i][0:1, 0:128],
                                 AF.Identity, scale=cst[0:1, 0:1]).then_inc(sems["mng"], 1)
                for sq in range(8):
                    g = 8 * c + sq
                    Ob = R[6 + sq % 2]
                    for jp in range(16):
                        P = 16 * sq + jp
                        if PAIR_EXP:
                            s.wait_ge(sems["smm"], 256 * c + 32 * sq + 2 * jp + 2)
                            spv_t = 256 * c + 2 * P - 2
                            if spv_t > 0:
                                s.wait_ge(sems["spv"], spv_t)
                            s.activation(Eb[:, (jp % 2) * 1024:(jp % 2) * 1024 + 1024],
                                         PS[:, 2048:3072], AF.Exp).then_inc(sems["sexp"], 1)
                        else:
                            for h in range(2):
                                jt = 2 * jp + h
                                TK = 32 * sq + jt
                                s.wait_ge(sems["smm"], 256 * c + TK + 1)
                                spv_t = 256 * c + TK - 3
                                if spv_t > 0:
                                    s.wait_ge(sems["spv"], spv_t)
                                s.activation(Eb[:, (jt % 4) * 512:(jt % 4) * 512 + 512],
                                             R[4 + h][:, 0:512], AF.Exp).then_inc(sems["sexp"], 1)
                        if sq < 7 and 12 <= jp <= 15:
                            qt2 = 4 * (sq + 1) + (jp - 12)
                            s.wait_ge(sems["mtr"], 32 * c + qt2 + 1)
                            s.activation(Qb[64:65, qt2 * 128:(qt2 + 1) * 128],
                                         R[4][0:1, 0:128], AF.Identity,
                                         scale=cst[0:1, 0:1]).then_inc(sems["mng"], 1)
                    if SROW_ACT:
                        s.wait_ge(sems["spv"], 256 * c + 32 * (sq + 1))
                        if g >= 2:
                            s.wait_ge(sems["pbs" + str(g % 2)], 16 * (g // 2))
                        s.activation(rsb[0:1, (g % 2) * 512:(g % 2) * 512 + 512],
                                     Ob[64:65, 0:512], AF.Copy).then_inc(sems["srow"], 1)
                for j in range(8):
                    bank = R[4 + j % 4]
                    s.wait_ge(sems["womm"], 8 * c + j + 1)
                    s.activation(t1c[:, j * 256:(j + 1) * 256], bank[:, 0:256],
                                 AF.Identity, scale=ga_v[:, j:j + 1],
                                 bias=gabo_v[:, j:j + 1]).then_inc(sems["oev"], 1)
        V["trev"] += 16 * NCH
        V["mng"] += 32 * NCH
        V["sexp"] += SEXP_CH * NCH
        V["srow"] += 8 * NCH
        V["oev"] += 8 * NCH

        @block.gpsimd
        def _(g_):
            g_.wait_ge(sems["bar"], V["bar"])
            g_.wait_ge(sems["mod"], MOD_DONE)
            for c in range(NCH):
                for sq in range(8):
                    g = 8 * c + sq
                    off = (g % 2) * 512
                    g_.wait_ge(sems["srow"], g + 1)
                    if g >= 2:
                        g_.wait_ge(sems["snrm"], 2 * (g - 1))
                    g_.dma_start(
                        out=Pbs[:, off:off + 512],
                        in_=rsb[0:1, off:off + 512].unsqueeze(1).to_broadcast((1, 64, 512)),
                    ).then_inc(sems["pbs" + str(g % 2)], 16)
        V["pbs0"] += 16 * 4 * NCH
        V["pbs1"] += 16 * 4 * NCH

        @block.vector
        def _(v):
            v.wait_ge(sems["bar"], V["bar"])
            v.wait_ge(sems["mod"], MOD_DONE)
            v.wait_ge(sems["vev"], VEV0)
            for c in range(NCH):
                if c % 2 == 0:
                    cp = c // 2
                    v.wait_ge(sems["lns"], 2)
                    if cp >= 1:
                        v.wait_ge(sems["tr"], TRW + 32 * cp)
                    for q in range(4):
                        i = 4 * cp + q
                        v.tensor_scalar(hn1[:, q * D:(q + 1) * D], x_sb[:, i * D:(i + 1) * D],
                                        mv[:, i * 2:i * 2 + 1], r_all[:, i:i + 1],
                                        ALU.subtract, ALU.mult)
                    v.drain().then_inc(sems["ln"], 1)
                for ft in range(24):
                    bank = R[4 + ft % 2]
                    G0 = (ft // 3) * 2
                    v.wait_ge(sems["qmm"], 24 * c + ft + 1)
                    if ft % 3 == 0:
                        v.tensor_scalar(Qb[0:64, G0 * 256:(G0 + 1) * 256], bank[0:64, 0:256],
                                        cst[0:64, 1:2], bq_eff[0:64, ft:ft + 1],
                                        ALU.mult, ALU.add).then_inc(sems["qev"], 1)
                        v.tensor_scalar(Kb[0:64, G0 * 256:(G0 + 1) * 256], bank[64:128, 0:256],
                                        bq_eff[64:128, ft:ft + 1], None,
                                        ALU.add).then_inc(sems["qev"], 1)
                    elif ft % 3 == 1:
                        v.tensor_scalar(Qb[0:64, (G0 + 1) * 256:(G0 + 2) * 256], bank[64:128, 0:256],
                                        cst[64:128, 1:2], bq_eff[64:128, ft:ft + 1],
                                        ALU.mult, ALU.add).then_inc(sems["qev"], 1)
                    else:
                        v.tensor_scalar(Kb[0:64, (G0 + 1) * 256:(G0 + 2) * 256], bank[0:64, 0:256],
                                        bq_eff[0:64, ft:ft + 1], None,
                                        ALU.add).then_inc(sems["qev"], 1)
                V["qev"] += QEV_CH
                for vb in range(4):
                    u, half = vb // 2, vb % 2
                    bank = R[6 + vb % 2]
                    v.wait_ge(sems["vmm"], 4 * c + vb + 1)
                    vt = Vtm[:].rearrange("p (t f) -> p t f", f=65)
                    dstv = vt[:, 16 * half + u: 16 * (half + 1): 2, 0:64]
                    v.tensor_copy(dstv, bank[:, 0:512].rearrange("p (g f) -> p g f", f=64)).then_inc(sems["vev"], 1)
                V["vev"] += 4
                for ev in attn_sched():
                    if ev[0] == "b":
                        sl = ev[1]
                        for qt in range(4 * sl, 4 * sl + 4):
                            for pc in range(4):
                                b = 4 * qt + pc
                                v.wait_ge(sems["s2m"], 256 * c + 2 * (b + 1))
                                v.reduce_max(MP4[:, b:b + 1],
                                             PS[:, (b % 2) * 1024:(b % 2) * 1024 + 1024],
                                             axis=mybir.AxisListType.X).then_inc(sems["s2r"], 1)
                            v.wait_ge(sems["s2r"], 128 * c + 4 * (qt + 1))
                            v.reduce_max(M_part[:, qt:qt + 1], MP4[:, 4 * qt:4 * qt + 4],
                                         axis=mybir.AxisListType.X).then_inc(sems["s2f"], 1)
                    if ev[0] == "s":
                        sq = ev[1]
                        g = 8 * c + sq
                        Obs = R[6 + sq % 2]
                        if SROW_ACT:
                            v.wait_ge(sems["pbs" + str(g % 2)], 16 * (g // 2 + 1))
                            if not INTERLEAVE:
                                v.drain()
                            v.reciprocal(Pbs[:, (g % 2) * 512:(g % 2) * 512 + 512],
                                         Pbs[:, (g % 2) * 512:(g % 2) * 512 + 512])
                            if not INTERLEAVE:
                                v.drain()
                        else:
                            v.wait_ge(sems["spv"], 256 * c + 32 * (sq + 1))
                            if g >= 2:
                                v.wait_ge(sems["pbs" + str(g % 2)], 16 * (g // 2))
                            v.reciprocal(rsb[0:1, (g % 2) * 512:(g % 2) * 512 + 512],
                                         Obs[64:65, 0:512]).then_inc(sems["srow"], 1)
                        if sq >= 1:
                            gq = g - 1
                            Obq = R[6 + (sq - 1) % 2]
                            if not SROW_ACT:
                                v.wait_ge(sems["pbs" + str(gq % 2)], 16 * (gq // 2 + 1))
                            v.tensor_mul(att[0:64, (sq - 1) * 256:sq * 256],
                                         Obq[0:64, 0:256],
                                         Pbs[:, (gq % 2) * 512:(gq % 2) * 512 + 256]).then_inc(sems["snrm"], 1)
                            v.tensor_mul(att[64:128, (sq - 1) * 256:sq * 256],
                                         Obq[0:64, 256:512],
                                         Pbs[:, (gq % 2) * 512 + 256:(gq % 2) * 512 + 512]).then_inc(sems["snrm"], 1)
                            V["snrm"] += 2
                V["s2r"] += 128
                V["s2f"] += 32
                g = 8 * c + 7
                v.drain()
                if not SROW_ACT:
                    v.wait_ge(sems["pbs" + str(g % 2)], 16 * (g // 2 + 1))
                v.tensor_mul(att[0:64, 7 * 256:8 * 256], R[7][0:64, 0:256],
                             Pbs[:, (g % 2) * 512:(g % 2) * 512 + 256]).then_inc(sems["snrm"], 1)
                v.tensor_mul(att[64:128, 7 * 256:8 * 256], R[7][0:64, 256:512],
                             Pbs[:, (g % 2) * 512 + 256:(g % 2) * 512 + 512]).then_inc(sems["snrm"], 1)
                V["snrm"] += 2
                for grp in range(4):
                    uh, jh = grp // 2, grp % 2
                    bank = R[4 + grp % 2]
                    v.wait_ge(sems["etr"], 16 * c + 4 * (grp + 1))
                    v.tensor_add(x_sb[:, (2 * c + uh) * D + jh * 512:(2 * c + uh) * D + jh * 512 + 512],
                                 x_sb[:, (2 * c + uh) * D + jh * 512:(2 * c + uh) * D + jh * 512 + 512],
                                 bank[:, 0:512]).then_inc(sems["eadd"], 1)
                V["eadd"] += 4
        V["ln"] += 4

        bar_all()
        ascope.close()
        wscope.close()

        if "xmid" in dbg:
            @block.sync
            def _(sync):
                tok["d_xmid"] = pool_s.issue(sync, dbg["xmid"].rearrange("(n p) f -> p n f", p=128),
                                             x_sb[:].rearrange("p (n f) -> p n f", f=D))
                tok["d_xmid_done"] = tok["d_xmid"]

        fscope = contextlib.ExitStack()
        h2fm = fscope.enter_context(nc.sbuf_tensor("h2fm", [128, 8 * 2048], BF16))
        we1h = fscope.enter_context(nc.sbuf_tensor("we1h", [128, 8 * 1024], BF16))
        we3h = fscope.enter_context(nc.sbuf_tensor("we3h", [128, 8 * 1024], BF16))
        we2h = fscope.enter_context(nc.sbuf_tensor("we2h", [128, 8 * 1024], BF16))
        gb_bc = fscope.enter_context(nc.sbuf_tensor("gb_bc", [128, D], BF16))
        g8 = fscope.enter_context(nc.sbuf_tensor("g8", [1, 1024], F32))
        g8b = fscope.enter_context(nc.sbuf_tensor("g8b", [1, 1024], BF16))
        H1 = fscope.enter_context(nc.psum_tensor("H1", [128, 2048], F32))
        H3 = fscope.enter_context(nc.psum_tensor("H3", [128, 2048], F32))

        H34 = [(H3, 0), (H3, 512), (H3, 1024), (H3, 1536)]

        @block.vector
        def _(v):
            for i in range(NT):
                v.bn_stats(st6[:, i * 12:i * 12 + 6], x_sb[:, i * D:i * D + 512])
                v.bn_stats(st6[:, i * 12 + 6:i * 12 + 12], x_sb[:, i * D + 512:(i + 1) * D])
            v.drain()
            for i in range(NT):
                v.bn_aggr(mv[:, i * 2:(i + 1) * 2], st6[:, i * 12:(i + 1) * 12])
            v.drain().then_inc(sems["ln2s"], 1)
        V["ln2s"] += 1

        @block.scalar
        def _(s):
            s.wait_ge(sems["ln2s"], 1)
            var_view = mv[:].rearrange("p (n two) -> p n two", two=2)[:, :, 1:2]
            s.activation(lnv[:], var_view, AF.Ln, bias=cst[:, 2:3])
            s.drain()
            s.activation(r_all[:], lnv[:], AF.Exp, scale=cst[:, 3:4]).then_inc(sems["ln2s"], 1)
        V["ln2s"] += 1

        @block.tensor
        def _(t):
            for k in range(8):
                t.transpose(H1[0:1, k * 128:(k + 1) * 128], gb_v[:, k:k + 1], identf[:])
            t.drain().then_inc(sems["gw"], 1)
        V["gw"] += 1

        @block.vector
        def _(v):
            v.wait_ge(sems["gw"], 1)
            v.tensor_copy(g8[:], H1[0:1, 0:1024])
            v.drain()
            v.tensor_copy(g8b[:], g8[:])
            v.drain().then_inc(sems["gw"], 1)
        V["gw"] += 1

        @block.tensor
        def _(t):
            t.wait_ge(sems["gw"], 2)
            for k in range(8):
                bank, off = (H3, (k % 4) * 512)
                t.matmul(bank[:, off + (k // 4) * 128: off + (k // 4) * 128 + 128],
                         ones_bf[:], g8b[0:1, k * 128:(k + 1) * 128], start=True, stop=True)
            t.drain().then_inc(sems["gw"], 1)
        V["gw"] += 1

        @block.vector
        def _(v):
            v.wait_ge(sems["gw"], 3)
            for k in range(8):
                bank, off = (H3, (k % 4) * 512)
                v.tensor_copy(gb_bc[:, k * 128:(k + 1) * 128],
                              bank[:, off + (k // 4) * 128: off + (k // 4) * 128 + 128])
            v.drain().then_inc(sems["gw"], 1)
        V["gw"] += 1
        GW0 = V["gw"]

        with nc.sbuf_tensor("hn2", [128, 4 * D], F32) as hn2:
            @block.vector
            def _(v):
                v.wait_ge(sems["ln2s"], 2)
                v.wait_ge(sems["gw"], GW0)
                for cp in range(4):
                    if cp >= 1:
                        v.wait_ge(sems["tr2"], 32 * cp)
                    for q in range(4):
                        i = 4 * cp + q
                        v.tensor_scalar(hn2[:, q * D:(q + 1) * D], x_sb[:, i * D:(i + 1) * D],
                                        mv[:, i * 2:i * 2 + 1], r_all[:, i:i + 1],
                                        ALU.subtract, ALU.mult)
                    v.drain().then_inc(sems["ln2"], 1)
            V["ln2"] += 4

            @block.tensor
            def _(t):
                t.wait_ge(sems["gw"], GW0)
                for cp in range(4):
                    t.wait_ge(sems["ln2"], cp + 1)
                    for k in range(8):
                        grp = cp * 8 + k
                        bank, off = H34[grp % 4]
                        if grp >= 4:
                            t.wait_ge(sems["trev2"], grp - 3)
                        for q in range(4):
                            t.transpose(bank[:, off + q * 128: off + (q + 1) * 128],
                                        hn2[:, q * D + k * 128: q * D + (k + 1) * 128],
                                        identf[:]).then_inc(sems["tr2"], 1)
            V["tr2"] += 128

            @block.scalar
            def _(s):
                for cp in range(4):
                    for k in range(8):
                        grp = cp * 8 + k
                        bank, off = H34[grp % 4]
                        s.wait_ge(sems["tr2"], 4 * (grp + 1))
                        s.activation(h2fm[:, k * 2048 + cp * 512: k * 2048 + (cp + 1) * 512],
                                     bank[:, off:off + 512], AF.Identity,
                                     scale=cb1[:, k:k + 1],
                                     bias=mod_sb[:, 24 + k:24 + k + 1]).then_inc(sems["trev2"], 1)
            V["trev2"] += 32

        with nc.sbuf_tensor("Gbuf", [128, 8 * 2048], BF16) as Gbuf, \
             nc.sbuf_tensor("gates", [128, NT * 8], F32) as gates, \
             nc.sbuf_tensor("top8", [128, NT * 8], F32) as top8, \
             nc.sbuf_tensor("gsum", [128, NT], F32) as gsum, \
             nc.sbuf_tensor("grcp", [128, NT], F32) as grcp, \
             nc.sbuf_tensor("gm1", [128, NT * 8], F32) as gm1, \
             nc.sbuf_tensor("gm2", [128, NT * 8], F32) as gm2, \
             nc.sbuf_tensor("w_sb", [128, NT * 8], F32) as w_sb:

            @block.tensor
            def _(t):
                t.wait_ge(sems["trev2"], 32)
                t.wait_ge(sems["gw"], 3)
                twait(t, "wg")
                for tc in range(NT):
                    bank, off = (H1, 0) if tc % 2 == 0 else (H1, 512)
                    if tc >= 2:
                        t.wait_ge(sems["gsig"], tc - 1)
                    mm = None
                    for k in range(8):
                        mm = t.matmul(bank[:, off:off + 8],
                                      h2fm[:, k * 2048 + tc * 128: k * 2048 + (tc + 1) * 128],
                                      wg_sb[:, k * 8:(k + 1) * 8], start=(k == 0), stop=(k == 7))
                    mm.then_inc(sems["gmm"], 1)
            V["gmm"] += NT

            @block.scalar
            def _(s):
                for tc in range(NT):
                    bank, off = (H1, 0) if tc % 2 == 0 else (H1, 512)
                    s.wait_ge(sems["gmm"], tc + 1)
                    s.activation(gates[:, tc * 8:(tc + 1) * 8], bank[:, off:off + 8],
                                 AF.Sigmoid).then_inc(sems["gsig"], 1)
            V["gsig"] += NT

            @block.vector
            def _(v):
                v.wait_ge(sems["gsig"], NT)
                for tc in range(NT):
                    v.max(top8[:, tc * 8:(tc + 1) * 8], gates[:, tc * 8:(tc + 1) * 8])
                v.drain()
                for tc in range(NT):
                    v.tensor_add(gsum[:, tc:tc + 1], top8[:, tc * 8:tc * 8 + 1],
                                 top8[:, tc * 8 + 1:tc * 8 + 2])
                v.drain()
                for tc in range(NT):
                    v.tensor_scalar(gsum[:, tc:tc + 1], gsum[:, tc:tc + 1], 3.0, None, ALU.mult)
                v.drain()
                for tc in range(NT):
                    v.reciprocal(grcp[:, tc:tc + 1], gsum[:, tc:tc + 1])
                for tc in range(NT):
                    v.tensor_scalar(gm1[:, tc * 8:(tc + 1) * 8], gates[:, tc * 8:(tc + 1) * 8],
                                    top8[:, tc * 8:tc * 8 + 1], None, ALU.is_equal)
                for tc in range(NT):
                    v.tensor_scalar(gm2[:, tc * 8:(tc + 1) * 8], gates[:, tc * 8:(tc + 1) * 8],
                                    top8[:, tc * 8 + 1:tc * 8 + 2], None, ALU.is_equal)
                v.drain()
                for tc in range(NT):
                    v.tensor_max(gm1[:, tc * 8:(tc + 1) * 8], gm1[:, tc * 8:(tc + 1) * 8],
                                 gm2[:, tc * 8:(tc + 1) * 8])
                v.drain()
                for tc in range(NT):
                    v.scalar_tensor_tensor(w_sb[:, tc * 8:(tc + 1) * 8],
                                           gates[:, tc * 8:(tc + 1) * 8], grcp[:, tc:tc + 1],
                                           gm1[:, tc * 8:(tc + 1) * 8], ALU.mult, ALU.mult)
                v.drain().then_inc(sems["gw"], 1)
            V["gw"] += 1
            GWW = V["gw"]

            if "wdbg" in dbg:
                @block.sync
                def _(sync):
                    sync.wait_ge(sems["gw"], GWW)
                    tok["d_wdbg"] = pool_s.issue(sync, dbg["wdbg"], w_sb[:])

            steps = [(e, fh) for e in range(9) for fh in range(2)]

            @block.gpsimd
            def _(g):
                for si, (e, fh) in enumerate(steps):
                    if si >= 1:
                        g.wait_ge(sems["fA"], 64 * si)
                    for nm, w9, buf in (("we1", we1_, we1h), ("we3", we3_, we3h)):
                        src = w9[e].rearrange("(k p) (h f) -> h p k f", p=128, f=1024, h=2)[fh]
                        tok[nm, si] = pool_g.issue(g, buf[:].rearrange("p (k f) -> p k f", f=1024), src)
                    if si >= 1:
                        g.wait_ge(sems["fB"], 32 * si)
                    tok["we2", si] = pool_g.issue(
                        g, we2h[:].rearrange("p (fc d) -> p fc d", d=1024),
                        we2_[e].rearrange("(h fc p) d -> h p fc d", p=128, fc=8)[fh])

            @block.tensor
            def _(t):
                t.wait_ge(sems["gsig"], NT)
                for si, (e, fh) in enumerate(steps):
                    twait(t, ("we1", si))
                    twait(t, ("we3", si))
                    if si >= 1:
                        t.wait_ge(sems["fEv"], si * 16)
                    for fc in range(8):
                        if si > 0 or fc > 0:
                            t.wait_ge(sems["fSl"], si * 8 + fc)
                        for nb in range(4):
                            mm = None
                            for k in range(8):
                                mm = t.matmul(H1[:, nb * 512:(nb + 1) * 512],
                                              we1h[:, k * 1024 + fc * 128: k * 1024 + (fc + 1) * 128],
                                              h2fm[:, k * 2048 + nb * 512: k * 2048 + (nb + 1) * 512],
                                              start=(k == 0), stop=(k == 7))
                            mm.then_inc(sems["fA"], 1)
                        if si > 0 or fc > 0:
                            t.wait_ge(sems["fG"], si * 8 + fc)
                        for nb in range(4):
                            mm = None
                            for k in range(8):
                                mm = t.matmul(H3[:, nb * 512:(nb + 1) * 512],
                                              we3h[:, k * 1024 + fc * 128: k * 1024 + (fc + 1) * 128],
                                              h2fm[:, k * 2048 + nb * 512: k * 2048 + (nb + 1) * 512],
                                              start=(k == 0), stop=(k == 7))
                            mm.then_inc(sems["fA"], 1)
                    t.wait_ge(sems["fG"], (si + 1) * 8)
                    t.wait_ge(sems["wsc"], si + 1)
                    for tc in range(NT):
                        off = 0 if tc % 2 == 0 else 1024
                        if tc >= 2:
                            t.wait_ge(sems["fEv"], si * 16 + tc - 1)
                        for nb in range(2):
                            mm = None
                            for fc in range(8):
                                mm = t.matmul(H1[:, off + nb * 512: off + (nb + 1) * 512],
                                              Gbuf[:, fc * 2048 + tc * 128: fc * 2048 + (tc + 1) * 128],
                                              we2h[:, fc * 1024 + nb * 512: fc * 1024 + (nb + 1) * 512],
                                              start=(fc == 0), stop=(fc == 7))
                            mm.then_inc(sems["fB"], 1)
            V["fA"] += len(steps) * 64
            V["fB"] += len(steps) * 32

            with nc.sbuf_tensor("slbuf", [128, 2 * 2048], BF16) as slbuf:
                @block.scalar
                def _(s):
                    for si, (e, fh) in enumerate(steps):
                        for fc in range(8):
                            s.wait_ge(sems["fA"], si * 64 + fc * 8 + 4)
                            if si * 8 + fc >= 2:
                                s.wait_ge(sems["fG"], si * 8 + fc - 1)
                            s.activation(slbuf[:, (fc % 2) * 2048:(fc % 2 + 1) * 2048],
                                         H1[:, 0:2048], AF.Silu).then_inc(sems["fSl"], 1)
                V["fSl"] += len(steps) * 8

                @block.vector
                def _(v):
                    v.wait_ge(sems["gw"], GW0)
                    for si, (e, fh) in enumerate(steps):
                        twait(v, ("we2", si))
                        for fc in range(8):
                            if e == 8:
                                v.scalar_tensor_tensor(we2h[:, fc * 1024:(fc + 1) * 1024],
                                                       we2h[:, fc * 1024:(fc + 1) * 1024],
                                                       cst[:, 4:5], gb_bc[:],
                                                       ALU.mult, ALU.mult)
                            else:
                                v.tensor_mul(we2h[:, fc * 1024:(fc + 1) * 1024],
                                             we2h[:, fc * 1024:(fc + 1) * 1024], gb_bc[:])
                        v.drain().then_inc(sems["wsc"], 1)
                        for fc in range(8):
                            v.wait_ge(sems["fSl"], si * 8 + fc + 1)
                            v.wait_ge(sems["fA"], si * 64 + fc * 8 + 8)
                            v.tensor_mul(Gbuf[:, fc * 2048:(fc + 1) * 2048],
                                         slbuf[:, (fc % 2) * 2048:(fc % 2 + 1) * 2048],
                                         H3[:, 0:2048]).then_inc(sems["fG"], 1)
                        for tc in range(NT):
                            off = 0 if tc % 2 == 0 else 1024
                            if si == 0 and tc == 0 and "xmid" in dbg:
                                twait(v, "d_xmid")
                            v.wait_ge(sems["fB"], si * 32 + 2 * (tc + 1))
                            if e == 8:
                                v.tensor_add(x_sb[:, tc * D:(tc + 1) * D],
                                             x_sb[:, tc * D:(tc + 1) * D],
                                             H1[:, off:off + 1024]).then_inc(sems["fEv"], 1)
                            else:
                                v.scalar_tensor_tensor(x_sb[:, tc * D:(tc + 1) * D],
                                                       H1[:, off:off + 1024],
                                                       w_sb[:, tc * 8 + e:tc * 8 + e + 1],
                                                       x_sb[:, tc * D:(tc + 1) * D],
                                                       ALU.mult, ALU.add).then_inc(sems["fEv"], 1)
                V["fG"] += len(steps) * 8
                V["fEv"] += len(steps) * 16
                V["wsc"] += len(steps)

        @block.sync
        def _(sync):
            sync.wait_ge(sems["fEv"], V["fEv"])
            sync.dma_start(out=out_.rearrange("(n p) f -> p n f", p=128),
                           in_=x_sb[:].rearrange("p (n f) -> p n f", f=D)).then_inc(sems["sto"], 16)
            sync.wait_ge(sems["sto"], 16)
            for key in ("d_mod48", "d_xmid", "d_wdbg"):
                if key in tok:
                    twait(sync, key)
        V["sto"] += 16

        fscope.close()
    return nc, dbg


def _prep_inputs(inputs, core):
    x = np.asarray(inputs["L0_attn_x"], dtype=np.float32).reshape(-1, D)
    c = np.asarray(inputs["L0_c"], dtype=np.float32)
    bqkv = np.asarray(inputs["L0_attn_bqkv"], dtype=np.float32)
    bqkvv = np.ascontiguousarray(bqkv.reshape(16, 192)[:, 128:].reshape(1, D))
    we1 = np.concatenate([np.asarray(inputs["L0_ffn_We1"], np.float32),
                          np.asarray(inputs["L0_ffn_Ws1"], np.float32)[None]], axis=0)
    we3 = np.concatenate([np.asarray(inputs["L0_ffn_We3"], np.float32),
                          np.asarray(inputs["L0_ffn_Ws3"], np.float32)[None]], axis=0)
    we2 = np.concatenate([np.asarray(inputs["L0_ffn_We2"], np.float32),
                          np.asarray(inputs["L0_ffn_Ws2"], np.float32)[None]], axis=0)
    return {
        "x": np.ascontiguousarray(x[core * T:(core + 1) * T]),
        "wqkv": np.ascontiguousarray(np.asarray(inputs["L0_attn_Wqkv"], np.float32)),
        "bqkv": np.ascontiguousarray(bqkv.reshape(24, 128).T),
        "bqkvv": bqkvv,
        "wo": np.ascontiguousarray(np.asarray(inputs["L0_attn_Wo"], np.float32)),
        "bo": np.ascontiguousarray(np.asarray(inputs["L0_attn_bo"], np.float32).reshape(8, 128).T),
        "wg": np.ascontiguousarray(np.asarray(inputs["L0_ffn_Wg"], np.float32).reshape(8, 128, 8).transpose(1, 0, 2).reshape(128, 64)),
        "we1": np.ascontiguousarray(we1),
        "we3": np.ascontiguousarray(we3),
        "we2": np.ascontiguousarray(we2),
        "cvec": np.ascontiguousarray(c[core // 2].reshape(8, 128).T),
        "wmod": np.ascontiguousarray(np.asarray(inputs["L0_wmod"], np.float32)),
        "ident": np.eye(128, dtype=ml_dtypes.bfloat16),
        "identf": np.eye(128, dtype=np.float32),
    }


def kernel(**inputs):
    nc, _ = build()
    in_maps = [_prep_inputs(inputs, core) for core in range(8)]
    res = run_bass_kernel_spmd(nc, in_maps, core_ids=list(range(8)))
    out = np.concatenate([np.asarray(r["out"]) for r in res.results], axis=0)
    return out.reshape(4, 4096, D).astype(np.float32)
